# version 22
# baseline (speedup 1.0000x reference)
import contextlib
import numpy as np
import ml_dtypes
import concourse.bass as bass
import concourse.mybir as mybir
from concourse.bass_utils import run_bass_kernel_spmd

F32 = mybir.dt.float32
BF16 = mybir.dt.bfloat16
AF = mybir.ActivationFunctionType
ALU = mybir.AluOpType
AXX = mybir.AxisListType.X
BFNP = ml_dtypes.bfloat16

B, N, D, H, DK, FF = 32, 2048, 1024, 16, 64, 4096
NCORES = 8
BCORE = B // NCORES
GA = 8
GB = 8

M1 = 12582912.0
M256 = 3221225472.0
CNORM = float(DK ** -0.25)
EPS_LN = 1e-5
EPS_ATT = 1e-6

CB_VALS = [CNORM, 4.0, 4.0 * M1, 2.0 ** -10, -3145728.0, 1.0, -4.0, -3.75,
           -49152.0, -49151.75, -3096580.0, 1e-5, 0.65536, 256.0]
(CB_CN, CB_4, CB_P4M1, CB_P10, CB_NM256S, CB_1, CB_N4, CB_N375,
 CB_NY0, CB_NY1, CB_SAB, CB_EPSLN, CB_EPS4, CB_256) = range(14)
NCB = 16


class Prog:
    ENGS = ("T", "V", "S", "G", "Y")
    SEMS = ("T", "V", "S", "G", "YI0", "YI1", "YS0", "YS1", "YA", "YB",
            "YX", "YX2", "YX3", "YX4", "W0", "W1", "S0", "S1")

    def __init__(self, nc):
        self.nc = nc
        self.sem = {}
        self.count = {e: 0 for e in self.SEMS}
        self.ops = {e: [] for e in self.ENGS}
        self.waited = {e: {} for e in self.ENGS}

    def make_sems(self, es):
        for e in self.SEMS:
            self.sem[e] = es.enter_context(self.nc.semaphore(f"sem{e}"))

    def emit(self, eng, fn, waits=(), inc=None):
        waits = list(waits)
        if eng in ("V", "S", "G"):
            waits.append((eng, self.count[eng]))
            if inc is None:
                inc = eng
        todo = []
        for t in waits:
            if t is None:
                continue
            we, wv = t
            if wv <= 0:
                continue
            assert wv <= self.count[we], f"future token {we}:{wv} (now {self.count[we]})"
            if self.waited[eng].get(we, 0) < wv:
                self.waited[eng][we] = wv
                todo.append((we, wv))
        tok = None
        if inc is True:
            inc = eng
        if inc:
            step = 16 if inc not in ("T", "V", "S", "G") else 1
            self.count[inc] += step
            tok = (inc, self.count[inc])
            sems = self.sem

            def run(h, fn=fn, todo=todo, sems=sems, inc=inc, step=step):
                for (we, v) in todo:
                    h.wait_ge(sems[we], v)
                fn(h).then_inc(sems[inc], step)
        else:
            sems = self.sem

            def run(h, fn=fn, todo=todo, sems=sems):
                for (we, v) in todo:
                    h.wait_ge(sems[we], v)
                fn(h)
        self.ops[eng].append(run)
        return tok

    def tok(self, eng):
        return (eng, self.count[eng])


DBG_NSEQ = BCORE
DBG_PHASE = 3
DBG_PA = 5
DBG_LN = 3


def build_program():
    nc = bass.Bass("TRN2", target_bir_lowering=False)
    dt = nc.dram_tensor

    xin = dt("xin", [BCORE, N, D], F32, kind="ExternalInput").ap()
    wq_t = dt("wq_t", [D, D], F32, kind="ExternalInput").ap()
    wk_t = dt("wk_t", [D, D], F32, kind="ExternalInput").ap()
    wv_t = dt("wv_t", [D, D], F32, kind="ExternalInput").ap()
    wo_t = dt("wo_t", [D, D], F32, kind="ExternalInput").ap()
    w1_d = dt("w1_d", [D, FF], F32, kind="ExternalInput").ap()
    w2_d = dt("w2_d", [FF, D], F32, kind="ExternalInput").ap()
    pbd_d = dt("pbd_d", [128, 128], F32, kind="ExternalInput").ap()
    bql_d = dt("bql_d", [128, 8], F32, kind="ExternalInput").ap()
    b1l_d = dt("b1l_d", [128, 32], F32, kind="ExternalInput").ap()
    bkl_d = dt("bkl_d", [128, 8], F32, kind="ExternalInput").ap()
    bvr_d = dt("bvr_d", [1, D], F32, kind="ExternalInput").ap()
    bor_d = dt("bor_d", [1, D], F32, kind="ExternalInput").ap()
    b1r_d = dt("b1r_d", [1, FF], F32, kind="ExternalInput").ap()
    b2r_d = dt("b2r_d", [1, D], F32, kind="ExternalInput").ap()
    cb_d = dt("cb_d", [128, NCB], F32, kind="ExternalInput").ap()
    idb_d = dt("idb_d", [128, 128], BF16, kind="ExternalInput").ap()
    idf_d = dt("idf_d", [128, 128], F32, kind="ExternalInput").ap()
    e16_d = dt("e16_d", [16, D], F32, kind="ExternalInput").ap()
    onesf_d = dt("onesf_d", [1, 512], F32, kind="ExternalInput").ap()
    onesb_d = dt("onesb_d", [128, 1], BF16, kind="ExternalInput").ap()
    yout = dt("yout", [BCORE, N, D], F32, kind="ExternalOutput").ap()
    wsc_q = dt("wsc_q", [128, 8 * D], BF16).ap()
    wsc_k = dt("wsc_k", [128, 8 * D], BF16).ap()
    wsc_v = dt("wsc_v", [128, 8 * D], BF16).ap()
    wsc_o = dt("wsc_o", [128, 8 * D], BF16).ap()
    wsc_2 = dt("wsc_2", [128, 32 * D], BF16).ap()

    es = contextlib.ExitStack()
    with es:
        sbt = lambda name, shape, dtp: es.enter_context(nc.sbuf_tensor(name, shape, dtp))
        pst_ = lambda name, shape, dtp: es.enter_context(nc.psum_tensor(name, shape, dtp))

        w1sb = sbt("w1sb", [128, 8 * FF], BF16)
        wa0 = sbt("wa0", [128, 8 * D], BF16)
        wa1 = sbt("wa1", [128, 8 * D], BF16)
        w2b0 = sbt("w2b0", [128, 2 * D], BF16)
        w2b1 = sbt("w2b1", [128, 2 * D], BF16)
        xbuf = sbt("xbuf", [128, 2048], F32)
        x2b = sbt("x2b", [128, 2048], F32)
        lnf = sbt("lnf", [128, 4096], BF16)
        poolB = sbt("poolB", [128, 10240], BF16)
        poolF = sbt("poolF", [128, 8080], F32)
        kvz = sbt("kvz", [128, 1024], BF16)
        ksz = sbt("ksz", [128, 16], BF16)
        lnstats = sbt("lnstats", [128, 32], F32)
        cb = sbt("cb", [128, NCB], F32)
        idb = sbt("idb", [128, 128], BF16)
        idf = sbt("idf", [128, 128], F32)
        shrow = sbt("shrow", [128, 5376], F32)
        e16 = shrow[0:16, 4352:4352 + D]
        ones0 = shrow[0:1, 4096:4352]
        onesf = shrow[32:33, 0:512]
        onesb = sbt("onesb", [128, 1], BF16)
        pbd = sbt("pbd", [128, 128], BF16)
        bql = sbt("bql", [128, 8], F32)
        bkl = sbt("bkl", [128, 8], F32)
        b1l = sbt("b1l", [128, 32], F32)
        b1l2 = sbt("b1l2", [128, 32], F32)
        cbq = sbt("cbq", [128, 8], F32)
        cbk = sbt("cbk", [128, 8], F32)
        bvr = shrow[32:33, 512:512 + D]
        bor = shrow[32:33, 1536:1536 + D]
        b1r = shrow[0:1, 0:FF]
        b2r = shrow[32:33, 2560:2560 + D]

        LN1TM = 0
        KFM = 2048
        KFTM = 4096
        VTM = 6144
        QFM = 2048
        QFFM = 4096
        AFM = 6144
        YQ = 8192
        CONVT = 0

        CH = [tuple(pb * 3072 + i * 512 for i in range(6)) for pb in range(2)]
        REX = 6144
        SQS = 6656
        LNT = 6656
        XI2 = 0
        DEN = 7680
        RDEN = 7744
        RD16 = 7808
        ST0 = 8064
        S_S1, S_S2A, S_S2B, S_M, S_MS, S_VAR, S_SD, S_R, S_NMR = range(9)

        def pf(slot, width=256):
            return poolF[:, slot:slot + width]

        def stat(i):
            return poolF[:, ST0 + i:ST0 + i + 1]

        psA = pst_("psA", [128, 512], F32)
        psB = pst_("psB", [128, 512], F32)
        psC = pst_("psC", [128, 512], F32)
        psD = pst_("psD", [128, 512], F32)
        psE = pst_("psE", [128, 512], F32)
        psF = pst_("psF", [128, 512], F32)
        psG = pst_("psG", [128, 512], F32)
        psT = pst_("psT", [128, 1024], BF16)

        p = Prog(nc)
        p.make_sems(es)
        state = {"sq_guard": None, "psT_guard": [], "tp_eng": True,
                 "sqs_guard": []}

        def ln_stats(srcs, eps_ap, scale_ap, bias_ap, wait_first=()):
            wv = list(wait_first)
            ntot = sum(s.shape[-1] for s in srcs)
            rtoks = []
            for i, s in enumerate(srcs):
                if i == 0:
                    rtoks.append(p.emit("V", lambda v, s=s: v.reduce_sum(
                        out=stat(S_S1), in_=s, axis=AXX), waits=wv, inc=True))
                else:
                    rtoks.append(p.emit("V", lambda v, s=s: v.reduce_sum(
                        out=stat(S_M), in_=s, axis=AXX), inc=True))
                    p.emit("V", lambda v: v.tensor_tensor(out=stat(S_S1), in0=stat(S_S1),
                                                          in1=stat(S_M), op=ALU.add))
            stoks = []
            for i, s in enumerate(srcs):
                acc = stat(S_S2A) if i == 0 else stat(S_S2B)
                gw = wv + ([state["sq_guard"]] if state["sq_guard"] else [])
                gw = gw + list(state["sqs_guard"])
                gw = gw + [rtoks[i]]
                stoks.append(p.emit("S", lambda s_, s=s, acc=acc: s_.activation(
                    out=poolF[:, SQS:SQS + s.shape[-1]], in_=s, func=AF.Square,
                    accum_out=acc), waits=gw, inc=True))
                if i > 0:
                    p.emit("V", lambda v: v.tensor_tensor(out=stat(S_S2A), in0=stat(S_S2A),
                                                          in1=stat(S_S2B), op=ALU.add),
                           waits=[stoks[-1]])
            vtok = p.emit("V", lambda v, ntot=ntot: v.tensor_scalar(
                out=poolF[:, ST0 + S_M:ST0 + S_MS + 1],
                in0=poolF[:, ST0 + S_S1:ST0 + S_S2A + 1],
                scalar1=1.0 / ntot, scalar2=None,
                op0=ALU.mult), waits=[stoks[-1]], inc=True)
            state["sq_guard"] = vtok
            p.emit("V", lambda v: v.tensor_tensor(out=stat(S_VAR), in0=stat(S_M),
                                                  in1=stat(S_M), op=ALU.mult))
            vtok2 = p.emit("V", lambda v: v.scalar_tensor_tensor(
                out=stat(S_VAR), in0=stat(S_VAR), scalar=-1.0, in1=stat(S_MS),
                op0=ALU.mult, op1=ALU.add), inc=True)
            sdt = p.emit("S", lambda s_, eps_ap=eps_ap: s_.activation(
                out=stat(S_SD), in_=stat(S_VAR), func=AF.Sqrt,
                scale=cb[:, CB_1:CB_1 + 1], bias=eps_ap), waits=[vtok2], inc=True)
            p.emit("V", lambda v, scale_ap=scale_ap: v.reciprocal(
                out=scale_ap, in_=stat(S_SD)), waits=[sdt])
            rtok = p.emit("V", lambda v, scale_ap=scale_ap, bias_ap=bias_ap: v.scalar_tensor_tensor(
                out=bias_ap, in0=stat(S_M), scalar=-1.0, in1=scale_ap,
                op0=ALU.mult, op1=ALU.mult), inc=True)
            return vtok, rtok

        def transpose_tile(src_slot, dst_base, dst_stride, t_idx, ntok):
            ttok = None
            for dc in range(8):
                ttok = p.emit("T", lambda tt, src_slot=src_slot, dc=dc: tt.transpose(
                    psT[:, dc * 128:(dc + 1) * 128],
                    in_=poolB[:, src_slot + dc * 128:src_slot + dc * 128 + 128],
                    identity=idb[:]),
                    waits=([ntok] + state["psT_guard"]) if dc == 0 else (),
                    inc=(dc == 7))
            eng = "V" if state["tp_eng"] else "S"
            state["tp_eng"] = not state["tp_eng"]
            assert dst_stride == 256
            base = dst_base + t_idx * 128
            full = lnf.rearrange("p (c j) -> p c j", j=256)
            c0, off = base // 256, base % 256
            dst = full[:, c0:c0 + 8, off:off + 128]
            src = psT[:].rearrange("p (c j) -> p c j", j=128)
            fn = ((lambda h, dst=dst, src=src: h.tensor_copy(out=dst, in_=src))
                  if eng == "V" else
                  (lambda h, dst=dst, src=src: h.copy(out=dst, in_=src)))
            tv = p.emit(eng, fn, waits=[ttok], inc=True)
            evs = [tv]
            state["psT_guard"] = evs
            return evs

        for (dst, src) in ((cb[:], cb_d), (idb[:], idb_d), (idf[:], idf_d),
                           (e16, e16_d), (onesf, onesf_d), (onesb[:], onesb_d),
                           (poolF[:, 0:128], pbd_d), (bql[:], bql_d), (bkl[:], bkl_d), (b1l[:], b1l_d),
                           (bvr, bvr_d), (bor, bor_d), (b1r, b1r_d), (b2r, b2r_d),
                           (ones0, onesf_d[0:1, 0:256])):
            p.emit("Y", lambda y, dst=dst, src=src: y.dma_start(out=dst, in_=src[:]),
                   inc="YI0")
        tok_small = p.tok("YI0")

        p.emit("V", lambda v: v.tensor_copy(out=pbd[:], in_=poolF[:, 0:128]), waits=[tok_small])
        p.emit("V", lambda v: v.tensor_scalar(out=b1l2[:], in0=b1l[:], scalar1=256.0,
                                              scalar2=None, op0=ALU.mult))
        p.emit("V", lambda v: v.tensor_scalar(out=cbq[:], in0=bql[:], scalar1=CNORM,
                                              scalar2=None, op0=ALU.mult))
        p.emit("V", lambda v: v.tensor_scalar(out=cbk[:], in0=bkl[:], scalar1=CNORM,
                                              scalar2=None, op0=ALU.mult), inc=True)
        tok_initv = p.tok("V")

        conv_jobs = []
        for (src, dstsc) in ((wq_t, wsc_q), (wk_t, wsc_k), (wv_t, wsc_v), (wo_t, wsc_o)):
            for dc in range(8):
                conv_jobs.append((src[dc * 128:(dc + 1) * 128, :], D,
                                  dstsc[:, dc * D:(dc + 1) * D], 1.0))
        for fc in range(32):
            conv_jobs.append((w2_d[fc * 128:(fc + 1) * 128, :], D,
                              wsc_2[:, fc * D:(fc + 1) * D], 1.0 / 256.0))
        prev_cast = {}
        prev_store = {}
        for i, (src, cols, dstsc, scale) in enumerate(conv_jobs):
            slot = (i % 2) * 1024
            eng = "Y" if i % 2 == 0 else "G"
            ltok = p.emit(eng, lambda h, src=src, slot=slot, cols=cols:
                          h.dma_start(out=poolF[:, slot:slot + cols], in_=src),
                          waits=[prev_cast.get(i - 2), tok_initv], inc="YI%d" % (i % 2))
            ctok = p.emit("V", lambda v, slot=slot, cols=cols, scale=scale:
                          v.tensor_scalar(out=poolB[:, CONVT + slot:CONVT + slot + cols],
                                          in0=poolF[:, slot:slot + cols],
                                          scalar1=scale, scalar2=None, op0=ALU.mult),
                          waits=[ltok, prev_store.get(i - 2)], inc=True)
            prev_cast[i] = ctok
            p.emit(eng, lambda h, slot=slot, cols=cols, dstsc=dstsc:
                   h.dma_start(out=dstsc, in_=poolB[:, CONVT + slot:CONVT + slot + cols]),
                   waits=[ctok], inc="YS%d" % (i % 2))
            prev_store[i] = p.tok("YS%d" % (i % 2))
        tok_scratch = (p.tok("YS0"), p.tok("YS1"))
        lastw1 = prev_cast[len(conv_jobs) - 1]
        for i in range(8):
            src = w1_d[i * 128:(i + 1) * 128, :]
            p.emit("Y", lambda y, src=src: y.dma_start(
                out=poolF[:, 0:FF // 2], in_=src[:, 0:FF // 2]),
                waits=[lastw1], inc="YI0")
            lt0 = p.tok("YI0")
            p.emit("G", lambda g_, src=src: g_.dma_start(
                out=poolF[:, FF // 2:FF], in_=src[:, FF // 2:FF]),
                waits=[lastw1], inc="YI1")
            lt1 = p.tok("YI1")
            lastw1 = p.emit("V", lambda v, i=i: v.tensor_copy(
                out=w1sb[:, i * FF:(i + 1) * FF], in_=poolF[:, 0:FF]),
                waits=[lt0, lt1], inc=True)
        tok_w1 = lastw1

        pending_final = None
        for b in range(min(BCORE, DBG_NSEQ) if DBG_PHASE >= 2 else 0):
            if pending_final is not None:
                fin_seq, _frest = pending_final()
                _frest()
                pending_final = None
            else:
                fin_seq = []
            kv_first = True
            vt_prev = ()
            kf_prev = None
            kt_hist = {}
            sub_idx = 0
            last_kv_tok = None

            PA_LNB = (LN1TM, 8192)
            PA_LNF = (0, 2048)
            pa_norm = {}
            pa_tr_tok = {}
            pa_lnfd = {}
            pa_xtok = {}

            def pa_xload(g, waits):
                xa = xbuf if g % 2 == 0 else x2b
                cls0 = "YX" if g % 2 == 0 else "YX3"
                cls1 = "YX2" if g % 2 == 0 else "YX4"
                p.emit("Y", lambda y, b=b, g=g, xa=xa: y.dma_start(
                    out=xa[:, 0:1024], in_=xin[b, g * 256:g * 256 + 128, :]),
                    waits=waits, inc=cls0)
                ta = p.tok(cls0)
                p.emit("G", lambda g_, b=b, g=g, xa=xa: g_.dma_start(
                    out=xa[:, 1024:2048], in_=xin[b, g * 256 + 128:g * 256 + 256, :]),
                    waits=waits, inc=cls1)
                tb = p.tok(cls1)
                pa_xtok[g] = (ta, tb)

            def pa_lnchain(g):
                xa = xbuf if g % 2 == 0 else x2b
                ta, tb = pa_xtok[g]
                ntoks = []
                for t in range(2):
                    xt = xa[:, t * 1024:(t + 1) * 1024]
                    kidx = 2 * (g * 2 + t)
                    _, rtok = ln_stats([xt], cb[:, CB_EPSLN:CB_EPSLN + 1],
                                       lnstats[:, kidx:kidx + 1],
                                       lnstats[:, kidx + 1:kidx + 2],
                                       wait_first=[ta if t == 0 else tb])
                    slotB = PA_LNB[g % 2] + t * 1024
                    ws = list(tok_scratch) if g <= 1 else []
                    ntok = p.emit("S", lambda s, xt=xt, kidx=kidx, slotB=slotB: s.activation(
                        out=poolB[:, slotB:slotB + 1024], in_=xt, func=AF.Identity,
                        bias=lnstats[:, kidx + 1:kidx + 2],
                        scale=lnstats[:, kidx:kidx + 1]),
                        waits=[rtok, pa_tr_tok.get(g - 2)] + ws, inc=True)
                    ntoks.append(ntok)
                pa_norm[g] = ntoks

            def pa_transposes(g):
                evs = []
                for t in range(2):
                    slotB = PA_LNB[g % 2] + t * 1024
                    evs.extend(transpose_tile(slotB, PA_LNF[g % 2], 256, t,
                                              pa_norm[g][t]))
                pa_tr_tok[g] = p.tok("T")
                pa_lnfd[g] = list(evs)

            pa_xload(0, [p.tok("V"), p.tok("S"), p.tok("G")])
            pa_xload(1, [p.tok("V"), p.tok("S"), p.tok("G")])
            tok_wa0 = p.emit("Y", lambda y: y.dma_start(out=wa0[:], in_=wsc_k[:]),
                             waits=[*tok_scratch, p.tok("T")], inc="YA")
            tok_wa1 = p.emit("G", lambda g_: g_.dma_start(out=wa1[:], in_=wsc_v[:]),
                             waits=[*tok_scratch, p.tok("T")], inc="YB")
            pa_lnchain(0)
            pa_transposes(0)

            for g in range(GA):
                if g + 2 < GA:
                    pa_xload(g + 2, pa_norm[g])
                lnf_base = PA_LNF[g % 2]
                lnf_done = pa_lnfd[g]

                if DBG_PA >= 2:
                    kevac = {}
                    for c in range(8):
                        bank = psA if c % 2 == 0 else psB
                        wv = list(lnf_done) + [tok_wa0, kevac.get(c - 2)] + fin_seq
                        mt = None
                        for dc in range(8):
                            mt = p.emit("T", lambda tt, bank=bank, c=c, dc=dc,
                                        lnf_base=lnf_base: tt.matmul(
                                bank[:, 0:256],
                                lhsT=wa0[:, dc * D + c * 128:dc * D + (c + 1) * 128],
                                rhs=lnf[:, lnf_base + dc * 256:lnf_base + (dc + 1) * 256],
                                start=(dc == 0), stop=(dc == 7)),
                                waits=wv if dc == 0 else (), inc=(dc == 7))
                        kevac[c] = p.emit("S", lambda s, bank=bank, c=c: s.activation(
                            out=poolB[:, KFM + c * 256:KFM + (c + 1) * 256], in_=bank[:, 0:256],
                            func=AF.Identity, scale=cb[:, CB_CN:CB_CN + 1], bias=cbk[:, c:c + 1]),
                            waits=[mt], inc=True)
                    kfm_done = [kevac[6], kevac[7]]

                if g + 1 < GA:
                    pa_lnchain(g + 1)

                for s in (range(2) if DBG_PA >= 3 else []):
                    kslot = KFTM + (s % 2) * 1024
                    wv = list(kfm_done) + [kf_prev]
                    ft = None
                    for gH in range(8):
                        bank = psA if gH < 4 else psB
                        col = (gH % 4) * 128
                        ft = p.emit("T", lambda tt, bank=bank, col=col, gH=gH, s=s: tt.matmul(
                            bank[:, col:col + 128],
                            lhsT=poolB[:, KFM + gH * 256 + s * 128:KFM + gH * 256 + s * 128 + 128],
                            rhs=pbd[:], start=True, stop=True),
                            waits=wv if gH == 0 else (), inc=(gH == 7))

                    vslot = VTM + (s % 2) * 1024
                    wv = list(lnf_done) + [tok_wa1, *vt_prev] + fin_seq
                    mt = None
                    for half, bank in ((0, psC), (1, psD)):
                        for dc in range(8):
                            mt = p.emit("T", lambda tt, bank=bank, s=s, dc=dc, half=half,
                                        lnf_base=lnf_base: tt.matmul(
                                bank[:],
                                lhsT=lnf[:, lnf_base + dc * 256 + s * 128:
                                         lnf_base + dc * 256 + s * 128 + 128],
                                rhs=wa1[:, dc * D + half * 512:dc * D + half * 512 + 512],
                                start=(dc == 0), stop=False),
                                waits=wv if (dc == 0 and half == 0) else (), inc=False)
                        mt = p.emit("T", lambda tt, bank=bank, half=half: tt.matmul(
                            bank[:], lhsT=onesf[:, 0:128],
                            rhs=bvr[:, half * 512:half * 512 + 512],
                            start=False, stop=True), inc=(half == 1))
                    vtv = p.emit("V", lambda v, vslot=vslot: v.tensor_copy(
                        out=poolB[:, vslot:vslot + 512], in_=psC[:]),
                        waits=[mt, kt_hist.get(sub_idx - 2)], inc=True)
                    vts = p.emit("S", lambda s_, vslot=vslot: s_.copy(
                        out=poolB[:, vslot + 512:vslot + 1024], in_=psD[:]),
                        waits=[mt, kt_hist.get(sub_idx - 2)], inc=True)
                    vt = (vtv, vts)
                    kf = None
                    for gH in range(8):
                        bank = psA if gH < 4 else psB
                        col = (gH % 4) * 128
                        kf = p.emit("V", lambda v, bank=bank, col=col, kslot=kslot, gH=gH: v.tensor_scalar(
                            out=poolB[:, kslot + gH * 128:kslot + gH * 128 + 128],
                            in0=bank[:, col:col + 128], scalar1=EPS_ATT, scalar2=EPS_ATT,
                            op0=ALU.add, op1=ALU.max),
                            waits=[ft, kt_hist.get(sub_idx - 2)] if gH == 0 else [ft],
                            inc=(gH == 7))
                    vt_prev, kf_prev = vt, kf

                    if DBG_PA < 4:
                        continue
                    last = (g == GA - 1 and s == 1)
                    kt = None
                    for gH in range(8):
                        bank = psF if gH < 4 else psG
                        col = (gH % 4) * 128
                        st_ = kv_first and (gH % 4 == 0)
                        sp_ = last and (gH % 4 == 3)
                        p.emit("T", lambda tt, bank=bank, col=col, kslot=kslot, vslot=vslot,
                               gH=gH, st_=st_, sp_=sp_: tt.matmul(
                            bank[:, col:col + 128],
                            lhsT=poolB[:, kslot + gH * 128:kslot + gH * 128 + 128],
                            rhs=poolB[:, vslot + gH * 128:vslot + gH * 128 + 128],
                            start=st_, stop=sp_),
                            waits=[kf, *vt] if gH == 0 else (), inc=False)
                        kt = p.emit("T", lambda tt, kslot=kslot, gH=gH,
                                    st_=(kv_first and gH == 0),
                                    sp_=(last and gH == 7): tt.matmul(
                            psE[:, gH:gH + 1],
                            lhsT=poolB[:, kslot + gH * 128:kslot + gH * 128 + 128],
                            rhs=onesb[:], start=st_, stop=sp_),
                            inc=(gH == 7))
                    kv_first = False
                    kt_hist[sub_idx] = kt
                    sub_idx += 1
                    last_kv_tok = kt

                if g + 1 < GA:
                    pa_transposes(g + 1)

            if DBG_PA < 5:
                continue
            p.emit("V", lambda v: v.memset(kvz[:], 0.0), waits=[last_kv_tok])
            p.emit("V", lambda v: v.memset(ksz[:], 0.0))
            kv_ready = None
            for gH in range(8):
                bank = psF if gH < 4 else psG
                col = (gH % 4) * 128
                p.emit("V", lambda v, bank=bank, col=col, gH=gH: v.tensor_copy(
                    out=kvz[0:64, gH * 128:gH * 128 + 64], in_=bank[0:64, col:col + 64]))
                p.emit("V", lambda v, bank=bank, col=col, gH=gH: v.tensor_copy(
                    out=kvz[64:128, gH * 128 + 64:gH * 128 + 128],
                    in_=bank[64:128, col + 64:col + 128]))
                p.emit("V", lambda v, gH=gH: v.tensor_copy(
                    out=ksz[0:64, 2 * gH:2 * gH + 1], in_=psE[0:64, gH:gH + 1]))
                kv_ready = p.emit("V", lambda v, gH=gH: v.tensor_copy(
                    out=ksz[64:128, 2 * gH + 1:2 * gH + 2], in_=psE[64:128, gH:gH + 1]),
                    inc=(gH == 7))

            if DBG_PHASE < 3:
                continue
            tok_wq = p.emit("Y", lambda y: y.dma_start(out=wa0[:], in_=wsc_q[:]),
                            waits=[*tok_scratch, p.tok("T")], inc="YA")
            tok_wo = p.emit("G", lambda g_: g_.dma_start(out=wa1[:], in_=wsc_o[:]),
                            waits=[*tok_scratch, p.tok("T")], inc="YB")

            w2_cons = {0: None, 1: None}
            store_tok = {}
            prev_xbuf_toks = []
            fin_prev = []
            prev_h_ev = []
            pre_prologue = None

            def emit_prologue_x(g8):
                xw = ([p.tok("V"), p.tok("S"), p.tok("G")] if g8 == 0 else
                      list(prev_xbuf_toks))
                p.emit("Y", lambda y, b=b, g8=g8: y.dma_start(
                    out=xbuf[:, 0:1024], in_=xin[b, g8 * 256:g8 * 256 + 128, :]),
                    waits=xw, inc="YX")
                txa = p.tok("YX")
                p.emit("G", lambda g_, b=b, g8=g8: g_.dma_start(
                    out=xbuf[:, 1024:2048], in_=xin[b, g8 * 256 + 128:g8 * 256 + 256, :]),
                    waits=xw, inc="YX2")
                txb = p.tok("YX2")
                return txa, txb

            def emit_prologue_norm(g8, txa, txb):
                pbg = list(state["psT_guard"])
                ln1_ntoks = []
                for t in range(2):
                    xt = xbuf[:, t * 1024:(t + 1) * 1024]
                    kidx = 2 * (g8 * 2 + t)
                    slotB = LN1TM + t * 1024
                    ntok = p.emit("S", lambda s, xt=xt, kidx=kidx, slotB=slotB: s.activation(
                        out=poolB[:, slotB:slotB + 1024], in_=xt, func=AF.Identity,
                        bias=lnstats[:, kidx + 1:kidx + 2],
                        scale=lnstats[:, kidx:kidx + 1]),
                        waits=[txa if t == 0 else txb, kv_ready] + pbg, inc=True)
                    ln1_ntoks.append(ntok)
                return ln1_ntoks

            def emit_prologue_tr(ln1_ntoks):
                lnf_done = []
                for t in range(2):
                    slotB = LN1TM + t * 1024
                    lnf_done = transpose_tile(slotB, 0, 256, t, ln1_ntoks[t])
                return lnf_done

            def make_att(lnfd, rd_prev):
                st = {"qev": {}, "qf_ev": {}, "rd_tok": None, "dt": None}

                def q_stage(c):
                    off = (c % 2) * 256
                    wv = list(lnfd) + [tok_wq, st["qev"].get(c - 1)]
                    if c == 0:
                        wv.append(rd_prev)
                    mt = None
                    for dc in range(8):
                        mt = p.emit("T", lambda tt, off=off, c=c, dc=dc: tt.matmul(
                            psG[:, off:off + 256],
                            lhsT=wa0[:, dc * D + c * 128:dc * D + (c + 1) * 128],
                            rhs=lnf[:, dc * 256:(dc + 1) * 256],
                            start=(dc == 0), stop=(dc == 7)),
                            waits=wv if dc == 0 else (), inc=(dc == 7))
                    st["qev"][c] = p.emit("S", lambda s, off=off, c=c: s.activation(
                        out=poolB[:, QFM + c * 256:QFM + (c + 1) * 256],
                        in_=psG[:, off:off + 256],
                        func=AF.Identity, scale=cb[:, CB_CN:CB_CN + 1],
                        bias=cbq[:, c:c + 1]), waits=[mt], inc=True)

                def qf_stage(gH):
                    off = (gH % 2) * 256
                    ft = p.emit("T", lambda tt, off=off, gH=gH: tt.matmul(
                        psG[:, off:off + 256], lhsT=pbd[:],
                        rhs=poolB[:, QFM + gH * 256:QFM + (gH + 1) * 256],
                        start=True, stop=True),
                        waits=[st["qev"][gH], st["qf_ev"].get(gH - 1)]
                        + ([st["qev"][7]] if gH == 0 else []), inc=True)
                    st["qf_ev"][gH] = p.emit("V", lambda v, off=off, gH=gH: v.tensor_scalar(
                        out=poolB[:, QFFM + gH * 256:QFFM + (gH + 1) * 256],
                        in0=psG[:, off:off + 256], scalar1=EPS_ATT, scalar2=EPS_ATT,
                        op0=ALU.add, op1=ALU.max),
                        waits=[ft] + ([st["qev"][7]] if gH == 0 else []), inc=True)

                def den_stage():
                    dt_ = None
                    for s in range(2):
                        for gH in range(8):
                            wv = [st["qf_ev"][gH]]
                            if s == 0 and gH == 0:
                                wv += [kv_ready, st["qf_ev"][7]]
                            dt_ = p.emit("T", lambda tt, s=s, gH=gH: tt.matmul(
                                psG[:, s * 16 + 2 * gH:s * 16 + 2 * gH + 2],
                                lhsT=poolB[:, QFFM + gH * 256 + s * 128:
                                           QFFM + gH * 256 + s * 128 + 128],
                                rhs=ksz[:, 2 * gH:2 * gH + 2],
                                start=(s == 0 and gH == 0),
                                stop=(s == 1 and gH == 7)),
                                waits=wv, inc=(s == 1 and gH == 7))
                    st["dt"] = dt_

                def rden_stage():
                    p.emit("V", lambda v: v.tensor_scalar(
                        out=pf(DEN, 32), in0=psG[:, 0:32],
                        scalar1=EPS_ATT, scalar2=None, op0=ALU.add),
                        waits=[st["dt"]])
                    den_tok = p.emit("V", lambda v: v.reciprocal(
                        out=pf(RDEN, 32), in_=pf(DEN, 32)), inc=True)
                    rt = None
                    for s in range(2):
                        rt = p.emit("T", lambda tt, s=s: tt.transpose(
                            psG[0:16, 64 + s * 128:64 + (s + 1) * 128],
                            in_=poolF[:, RDEN + s * 16:RDEN + (s + 1) * 16],
                            identity=idf[:]), waits=[den_tok] if s == 0 else (),
                            inc=(s == 1))
                    st["rd_tok"] = p.emit("V", lambda v: v.tensor_copy(
                        out=poolF[0:16, RD16:RD16 + 256], in_=psG[0:16, 64:320]),
                        waits=[rt], inc=True)

                def rex_stage(gH, guard=()):
                    bankR = psE if gH % 2 == 0 else psF
                    st["rex_mt"][gH] = p.emit("T", lambda tt, bankR=bankR, gH=gH: tt.matmul(
                        bankR[:, 0:256], lhsT=e16[:, gH * 128:(gH + 1) * 128],
                        rhs=poolF[0:16, RD16:RD16 + 256], start=True, stop=True),
                        waits=[st["rd_tok"]] + list(guard), inc=True)

                st["rex_mt"] = {}
                st["rex_fn"] = rex_stage
                return st, q_stage, qf_stage, den_stage, rden_stage, rex_stage

            att_pre = None
            att_stage_fns = None

            for g8 in range(GB):
                tk0 = g8 * 2
                w2tok = {}
                for e8 in range(2):
                    dst = w2b0 if e8 % 2 == 0 else w2b1
                    w2tok[e8] = p.emit("Y", lambda y, dst=dst, e8=e8: y.dma_start(
                        out=dst[:], in_=wsc_2[:, e8 * 2 * D:(e8 + 1) * 2 * D]),
                        waits=[*tok_scratch, w2_cons[e8 % 2]], inc="W%d" % (e8 % 2))

                if pre_prologue is not None:
                    ln1_ntoks, lnf_done = pre_prologue
                    pre_prologue = None
                else:
                    txa, txb = emit_prologue_x(g8)
                    ln1_ntoks = emit_prologue_norm(g8, txa, txb)
                    lnf_done = emit_prologue_tr(ln1_ntoks)

                if att_pre is not None:
                    att = att_pre
                    att_pre = None
                else:
                    att, q_s, qf_s, den_s, rden_s, rex_s0 = make_att(lnf_done, kv_ready)
                    for c in range(8):
                        q_s(c)
                    for gH in range(8):
                        qf_s(gH)
                    den_s()
                    rden_s()
                qf_ev = att["qf_ev"]
                rd_tok = att["rd_tok"]

                fin_rest = None
                if pending_final is not None:
                    fin_prev, fin_rest = pending_final()
                    pending_final = None

                a_ev = {}
                rex_s = att["rex_fn"]
                for gH in range(8):
                    rex_s(gH, [a_ev.get(gH - 2)]
                          + (prev_h_ev if gH <= 1 else []))
                    bankR = psE if gH % 2 == 0 else psF
                    bankA = psC if gH % 2 == 0 else psD
                    rslot = REX + (gH % 2) * 256
                    rc = p.emit("V" if gH % 2 == 0 else "S",
                                (lambda v, bankR=bankR, rslot=rslot: v.tensor_copy(
                                    out=pf(rslot, 256), in_=bankR[:, 0:256]))
                                if gH % 2 == 0 else
                                (lambda s_, bankR=bankR, rslot=rslot: s_.copy(
                                    out=pf(rslot, 256), in_=bankR[:, 0:256])),
                        waits=[att["rex_mt"][gH], a_ev.get(gH - 2)], inc=True)
                    at = p.emit("T", lambda tt, bankA=bankA, gH=gH: tt.matmul(
                        bankA[:, 0:256], lhsT=kvz[:, gH * 128:(gH + 1) * 128],
                        rhs=poolB[:, QFFM + gH * 256:QFFM + (gH + 1) * 256],
                        start=True, stop=True),
                        waits=[qf_ev[gH], kv_ready, a_ev.get(gH - 2),
                               qf_ev.get(7) if gH <= 1 else None]
                        + (fin_prev[:1] if gH <= 1 else []), inc=True)
                    a_ev[gH] = p.emit("V", lambda v, bankA=bankA, rslot=rslot, gH=gH: v.tensor_tensor(
                        out=poolB[:, AFM + gH * 256:AFM + (gH + 1) * 256],
                        in0=bankA[:, 0:256], in1=pf(rslot, 256), op=ALU.mult),
                        waits=[at, rc], inc=True)

                if fin_rest is not None:
                    fin_rest()

                ffn_guard = []
                fg_s = {}
                x2_toks = []
                wo_mt = {}
                for s in range(2):
                    b0 = psA if s == 0 else psC
                    b1_ = psB if s == 0 else psD
                    wv = [a_ev[7], tok_wo] + fin_prev
                    mt = None
                    for half, bank in ((0, b0), (1, b1_)):
                        for dc in range(8):
                            mt = p.emit("T", lambda tt, bank=bank, s=s, dc=dc, half=half: tt.matmul(
                                bank[:],
                                lhsT=poolB[:, AFM + dc * 256 + s * 128:AFM + dc * 256 + s * 128 + 128],
                                rhs=wa1[:, dc * D + half * 512:dc * D + half * 512 + 512],
                                start=(dc == 0), stop=False),
                                waits=wv if (dc == 0 and half == 0) else (), inc=False)
                        mt = p.emit("T", lambda tt, bank=bank, half=half: tt.matmul(
                            bank[:], lhsT=onesf[:, 0:128],
                            rhs=bor[:, half * 512:half * 512 + 512],
                            start=False, stop=True), inc=(half == 1))
                    wo_mt[s] = mt
                for s in range(2):
                    b0 = psA if s == 0 else psC
                    b1_ = psB if s == 0 else psD
                    _, rtok = ln_stats([b0[:], b1_[:]], cb[:, CB_EPSLN:CB_EPSLN + 1], stat(S_R), stat(S_NMR),
                                       wait_first=[wo_mt[s]])
                    n1 = p.emit("S", lambda s_, b0=b0: s_.activation(
                        out=pf(LNT, 512), in_=b0[:], func=AF.Identity,
                        scale=stat(S_R), bias=stat(S_NMR)),
                        waits=[rtok] + list(state["sqs_guard"]), inc=True)
                    n2 = p.emit("S", lambda s_, b1_=b1_: s_.activation(
                        out=poolF[:, LNT + 512:LNT + 1024], in_=b1_[:], func=AF.Identity,
                        scale=stat(S_R), bias=stat(S_NMR)), inc=True)
                    x2t = p.emit("V", lambda v, s=s: v.tensor_tensor(
                        out=x2b[:, s * 1024:(s + 1) * 1024], in0=pf(LNT, 1024),
                        in1=xbuf[:, s * 1024:(s + 1) * 1024], op=ALU.add),
                        waits=[n2], inc=True)
                    state["sqs_guard"] = [x2t]
                    ffn_guard.append(x2t)
                    x2_toks.append(x2t)

                    x2ap = x2b[:, s * 1024:(s + 1) * 1024]
                    _, rtok = ln_stats([x2ap], cb[:, CB_EPSLN:CB_EPSLN + 1], stat(S_R), stat(S_NMR),
                                       wait_first=[x2t])
                    slotB = LN1TM + s * 1024
                    ntok = p.emit("S", lambda s_, x2ap=x2ap, slotB=slotB: s_.activation(
                        out=poolB[:, slotB:slotB + 1024], in_=x2ap, func=AF.Identity,
                        scale=stat(S_R), bias=stat(S_NMR)), waits=[rtok], inc=True)
                    evs = transpose_tile(slotB, 2048, 256, s, ntok)
                    ffn_guard.extend(evs)
                    fg_s[s] = [x2t] + list(evs)
                    if s == 0:
                        for prh in range(2):
                            hbk = psE if prh % 2 == 0 else psF
                            hw = list(fg_s[0]) + [tok_w1, a_ev[7]]
                            for sub in range(2):
                                fc = 2 * prh + sub
                                for dc in range(8):
                                    p.emit("T", lambda tt, hbk=hbk, fc=fc, dc=dc,
                                           sub=sub: tt.matmul(
                                        hbk[:, sub * 256:sub * 256 + 128],
                                        lhsT=w1sb[:, dc * FF + fc * 128:dc * FF + fc * 128 + 128],
                                        rhs=lnf[:, 2048 + dc * 256:2048 + dc * 256 + 128],
                                        start=(sub == 0 and dc == 0), stop=False),
                                        waits=hw if (sub == 0 and dc == 0) else (),
                                        inc=False)

                h_ev = {}
                chain_tok = {}
                po_hist = {}
                po_done = None
                back_p4 = {}

                def emit_back(bpr):
                    SL = CH[bpr % 2]
                    y2t = p.emit("V", lambda v, SL=SL: v.scalar_tensor_tensor(
                        out=pf(SL[3], 512), in0=pf(SL[4], 512), scalar=256.0,
                        in1=pf(SL[1], 512), op0=ALU.mult, op1=ALU.add),
                        waits=[back_p4[bpr]], inc=True)
                    yslot = YQ + (bpr % 4) * 512
                    yqt = p.emit("V", lambda v, yslot=yslot, SL=SL: v.tensor_scalar(
                        out=poolB[:, yslot:yslot + 512], in0=pf(SL[3], 512), scalar1=M1,
                        scalar2=M1, op0=ALU.add, op1=ALU.subtract),
                        waits=[y2t, po_hist.get(bpr - 4)], inc=True)
                    chain_tok[bpr] = yqt
                def emit_po(pr_po):
                    wbuf = w2b0 if pr_po % 2 == 0 else w2b1
                    yslot = YQ + (pr_po % 4) * 512
                    pd = None
                    for sub in range(2):
                        for s2 in range(2):
                            for half in range(2):
                                bank = (psA, psB, psC, psD)[s2 * 2 + half]
                                pd = p.emit("T", lambda tt, bank=bank, wbuf=wbuf,
                                            yslot=yslot, sub=sub, s2=s2, half=half,
                                            pr_po=pr_po: tt.matmul(
                                    bank[:],
                                    lhsT=poolB[:, yslot + sub * 256 + s2 * 128:
                                               yslot + sub * 256 + s2 * 128 + 128],
                                    rhs=wbuf[:, sub * D + half * 512:
                                             sub * D + half * 512 + 512],
                                    start=(pr_po == 0 and sub == 0), stop=False),
                                    waits=[chain_tok[pr_po], w2tok[pr_po]]
                                    if (sub == 0 and s2 == 0 and half == 0) else (),
                                    inc=(sub == 1 and s2 == 1 and half == 1))
                    po_hist[pr_po] = pd
                    w2_cons[pr_po % 2] = pd
                    return pd

                px_next = None
                pro_norm_next = None
                if g8 + 1 < GB:
                    prev_xbuf_toks = x2_toks + ln1_ntoks
                    px_next = emit_prologue_x(g8 + 1)

                def att_late(it):
                    if g8 + 1 >= GB:
                        return
                    if it in (4, 6, 8, 10, 12, 14):
                        att_stage_fns[0]((it - 4) // 2)
                    elif it in (15, 16):
                        att_stage_fns[0](it - 9)
                    elif it in (17, 18):
                        for gH in range(4 * (it - 17), 4 * (it - 16)):
                            att_stage_fns[1](gH)
                    elif it == 19:
                        att_stage_fns[2]()
                        att_stage_fns[3]()

                for it in range(20):
                    if g8 + 1 < GB and it == 2:
                        pro_norm_next = emit_prologue_norm(g8 + 1, *px_next)
                    if g8 + 1 < GB and it == 3:
                        lnfd_next = emit_prologue_tr(pro_norm_next)
                        pre_prologue = (pro_norm_next, lnfd_next)
                        att_n, q_s, qf_s, den_s, rden_s, rex_s = make_att(
                            lnfd_next, att["rd_tok"])
                        att_stage_fns = (q_s, qf_s, den_s, rden_s, rex_s)
                    if it >= 4:
                        po_done = emit_po(it - 4)
                    prl = it - 2
                    if 2 <= prl < 16:
                        dst = w2b0 if prl % 2 == 0 else w2b1
                        w2tok[prl] = p.emit("Y", lambda y, dst=dst, prl=prl: y.dma_start(
                            out=dst[:], in_=wsc_2[:, prl * 2 * D:(prl + 1) * 2 * D]),
                            waits=[w2_cons[prl % 2]], inc="W%d" % (prl % 2))
                    if it >= 16:
                        if it == 16:
                            emit_back(15)
                        att_late(it)
                        if it == 19 and g8 + 1 < GB:
                            att_pre = att_n
                        continue
                    pr = it
                    SL_A, SL_B, SL_C, SL_D, SL_E, SL_F = CH[pr % 2]
                    hbank = psE if pr % 2 == 0 else psF
                    mt = None
                    if pr <= 1:
                        wv = list(fg_s[1]) + [tok_w1]
                        for sub in range(2):
                            fc = 2 * pr + sub
                            for dc in range(8):
                                mt = p.emit("T", lambda tt, hbank=hbank, fc=fc, dc=dc,
                                            sub=sub: tt.matmul(
                                    hbank[:, sub * 256 + 128:sub * 256 + 256],
                                    lhsT=w1sb[:, dc * FF + fc * 128:dc * FF + fc * 128 + 128],
                                    rhs=lnf[:, 2048 + dc * 256 + 128:2048 + (dc + 1) * 256],
                                    start=False,
                                    stop=(sub == 1 and dc == 7)),
                                    waits=wv if (sub == 0 and dc == 0) else (),
                                    inc=(sub == 1 and dc == 7))
                    else:
                        wv = list(ffn_guard) + [tok_w1, h_ev.get(pr - 2)]
                        for sub in range(2):
                            fc = 2 * pr + sub
                            for dc in range(8):
                                mt = p.emit("T", lambda tt, hbank=hbank, fc=fc, dc=dc,
                                            sub=sub: tt.matmul(
                                    hbank[:, sub * 256:sub * 256 + 256],
                                    lhsT=w1sb[:, dc * FF + fc * 128:dc * FF + fc * 128 + 128],
                                    rhs=lnf[:, 2048 + dc * 256:2048 + (dc + 1) * 256],
                                    start=(sub == 0 and dc == 0),
                                    stop=(sub == 1 and dc == 7)),
                                    waits=wv if (sub == 0 and dc == 0) else (),
                                    inc=(sub == 1 and dc == 7))
                    guard = [chain_tok.get(pr - 2)]
                    if pr == 0:
                        guard += [store_tok.get(0), store_tok.get(1)]
                    t1a = p.emit("S", lambda s_, hbank=hbank, SL_A=SL_A, pr=pr: s_.activation(
                        out=pf(SL_A, 256), in_=hbank[:, 0:256], func=AF.Identity,
                        scale=cb[:, CB_256:CB_256 + 1],
                        bias=b1l2[:, 2 * pr:2 * pr + 1]),
                        waits=[mt] + guard, inc=True)
                    t1t = p.emit("S", lambda s_, hbank=hbank, SL_A=SL_A, pr=pr: s_.activation(
                        out=poolF[:, SL_A + 256:SL_A + 512], in_=hbank[:, 256:512],
                        func=AF.Identity, scale=cb[:, CB_256:CB_256 + 1],
                        bias=b1l2[:, 2 * pr + 1:2 * pr + 2]), inc=True)
                    h_ev[pr] = t1t
                    xi4t = p.emit("V", lambda v, SL_A=SL_A, SL_B=SL_B: v.tensor_scalar(
                        out=pf(SL_B, 512), in0=pf(SL_A, 512), scalar1=4.0,
                        scalar2=4.0 * M1, op0=ALU.mult, op1=ALU.add),
                        waits=[t1t] + guard, inc=True)
                    t2t = p.emit("V", lambda v, SL_C=SL_C, SL_B=SL_B: v.tensor_scalar(
                        out=pf(SL_C, 512), in0=pf(SL_B, 512),
                        scalar1=3968.0 - 4.0 * M1, scalar2=M256,
                        op0=ALU.add, op1=ALU.add), waits=[xi4t], inc=True)
                    sat = p.emit("V", lambda v, SL_D=SL_D, SL_C=SL_C: v.tensor_scalar(
                        out=pf(SL_D, 512), in0=pf(SL_C, 512), scalar1=2.0 ** -10,
                        scalar2=-3096580.0, op0=ALU.mult, op1=ALU.add),
                        waits=[t2t], inc=True)
                    y0t = p.emit("S", lambda s_, SL_E=SL_E, SL_D=SL_D: s_.activation(
                        out=pf(SL_E, 512), in_=pf(SL_D, 512), func=AF.Gelu,
                        scale=cb[:, CB_1:CB_1 + 1], bias=cb[:, CB_NY0:CB_NY0 + 1]),
                        waits=[sat], inc=True)
                    y1t = p.emit("S", lambda s_, SL_F=SL_F, SL_D=SL_D: s_.activation(
                        out=pf(SL_F, 512), in_=pf(SL_D, 512), func=AF.Gelu,
                        scale=cb[:, CB_1:CB_1 + 1], bias=cb[:, CB_NY1:CB_NY1 + 1]), inc=True)
                    dtk = p.emit("G", lambda g_, SL_A=SL_A, SL_F=SL_F, SL_E=SL_E: g_.tensor_tensor(
                        out=pf(SL_A, 512), in0=pf(SL_F, 512), in1=pf(SL_E, 512),
                        op=ALU.subtract),
                        waits=[y1t, chain_tok.get(pr - 2)], inc=True)
                    m4t = p.emit("V", lambda v, SL_C=SL_C, SL_D=SL_D, SL_B=SL_B: v.scalar_tensor_tensor(
                        out=pf(SL_C, 512), in0=pf(SL_D, 512), scalar=-1024.0,
                        in1=pf(SL_B, 512), op0=ALU.mult, op1=ALU.add),
                        waits=[sat], inc=True)
                    p4t = p.emit("G", lambda g_, SL_B=SL_B, SL_C=SL_C, SL_A=SL_A: g_.tensor_tensor(
                        out=pf(SL_B, 512), in0=pf(SL_C, 512), in1=pf(SL_A, 512),
                        op=ALU.mult),
                        waits=[m4t], inc=True)
                    back_p4[pr] = p4t
                    if pr >= 1:
                        emit_back(pr - 1)
                    att_late(it)

                def emit_final(po_done=po_done, g8=g8, b=b):
                    fin_toks = []
                    xt2s = {}
                    bt = None
                    for s2 in range(2):
                        for half in range(2):
                            bank = (psA, psB, psC, psD)[s2 * 2 + half]
                            bt = p.emit("T", lambda tt, bank=bank, half=half: tt.matmul(
                                bank[:], lhsT=onesf[:, 0:128],
                                rhs=b2r[:, half * 512:half * 512 + 512],
                                start=False, stop=True),
                                waits=[po_done] if (s2 == 0 and half == 0) else (),
                                inc=(s2 == 1 and half == 1))
                    for s2 in (1, 0):
                        b0 = psA if s2 == 0 else psC
                        b1_ = psB if s2 == 0 else psD
                        xi2ap = poolF[:, XI2 + s2 * 1024:XI2 + (s2 + 1) * 1024]
                        p.emit("V", lambda v, b0=b0, xi2ap=xi2ap: v.tensor_scalar(
                            out=xi2ap[:, 0:512], in0=b0[:], scalar1=256.0, scalar2=M1,
                            op0=ALU.mult, op1=ALU.add),
                               waits=[bt, store_tok.get(s2)])
                        p.emit("V", lambda v, b1_=b1_, xi2ap=xi2ap: v.tensor_scalar(
                            out=xi2ap[:, 512:1024], in0=b1_[:], scalar1=256.0, scalar2=M1,
                            op0=ALU.mult, op1=ALU.add))
                        xt2 = p.emit("V", lambda v, xi2ap=xi2ap: v.tensor_scalar(
                            out=xi2ap, in0=xi2ap, scalar1=M1, scalar2=None,
                            op0=ALU.subtract), inc=True)
                        fin_toks.append(xt2)
                        xt2s[s2] = xt2

                    def rest():
                        for s2 in (1, 0):
                            xi2ap = poolF[:, XI2 + s2 * 1024:XI2 + (s2 + 1) * 1024]
                            _, rtok = ln_stats([xi2ap], cb[:, CB_EPS4:CB_EPS4 + 1],
                                               stat(S_R), stat(S_NMR),
                                               wait_first=[xt2s[s2]])
                            n4 = p.emit("S", lambda s_, xi2ap=xi2ap: s_.activation(
                                out=pf(LNT, 1024), in_=xi2ap, func=AF.Identity,
                                scale=stat(S_R), bias=stat(S_NMR)),
                                waits=[rtok] + list(state["sqs_guard"]), inc=True)
                            outap = xi2ap
                            ot = p.emit("V", lambda v, s2=s2, outap=outap: v.tensor_tensor(
                                out=outap, in0=pf(LNT, 1024),
                                in1=x2b[:, s2 * 1024:(s2 + 1) * 1024], op=ALU.add),
                                waits=[n4], inc=True)
                            tok0 = g8 * 256 + s2 * 128
                            eng = "Y" if s2 == 0 else "G"
                            store_tok[s2] = p.emit(
                                eng, lambda h, b=b, tok0=tok0, outap=outap: h.dma_start(
                                    out=yout[b, tok0:tok0 + 128, :], in_=outap),
                                waits=[ot], inc="S%d" % s2)
                    return fin_toks, rest
                pending_final = emit_final
                prev_xbuf_toks = x2_toks + ln1_ntoks
                prev_h_ev = [h_ev.get(14), h_ev.get(15)]


        if pending_final is not None:
            _ft, _frest = pending_final()
            _frest()
            pending_final = None

        f0, f1 = p.count["S0"], p.count["S1"]
        p.ops["Y"].append(lambda h, f0=f0, f1=f1: (
            h.wait_ge(p.sem["S0"], f0) if f0 else None,
            h.wait_ge(p.sem["S1"], f1) if f1 else None))

        block = es.enter_context(nc.Block())

        @block.tensor
        def _(t):
            for f in p.ops["T"]:
                f(t)

        @block.vector
        def _(v):
            for f in p.ops["V"]:
                f(v)

        @block.scalar
        def _(s):
            for f in p.ops["S"]:
                f(s)

        @block.gpsimd
        def _(g):
            for f in p.ops["G"]:
                f(g)

        @block.sync
        def _(y):
            for f in p.ops["Y"]:
                f(y)

    return nc


def _host_inputs(inputs):
    f32 = np.float32
    asc = np.ascontiguousarray
    Wq = inputs["L0_attn_Wq"]; Wk = inputs["L0_attn_Wk"]
    Wv = inputs["L0_attn_Wv"]; Wo = inputs["L0_attn_Wo"]
    P_ = np.asarray(inputs["L0_attn_P"], f32)
    pbd = np.zeros((128, 128), f32)
    pbd[0:64, 0:64] = P_.T
    pbd[64:128, 64:128] = P_.T
    e16 = np.zeros((16, D), f32)
    for h in range(16):
        e16[h, h * 64:(h + 1) * 64] = 1.0
    cbv = np.zeros(NCB, f32)
    cbv[:len(CB_VALS)] = CB_VALS
    common = {
        "wq_t": asc(np.asarray(Wq, f32).T), "wk_t": asc(np.asarray(Wk, f32).T),
        "wv_t": asc(np.asarray(Wv, f32).T), "wo_t": asc(np.asarray(Wo, f32).T),
        "w1_d": asc(np.asarray(inputs["L0_ffn_w1"], f32)),
        "w2_d": asc(np.asarray(inputs["L0_ffn_w2"], f32)),
        "pbd_d": pbd,
        "bql_d": asc(np.asarray(inputs["L0_attn_bq"], f32).reshape(8, 128).T),
        "b1l_d": asc(np.asarray(inputs["L0_ffn_b1"], f32).reshape(32, 128).T),
        "bkl_d": asc(np.asarray(inputs["L0_attn_bk"], f32).reshape(8, 128).T),
        "bvr_d": np.asarray(inputs["L0_attn_bv"], f32).reshape(1, D),
        "bor_d": np.asarray(inputs["L0_attn_bo"], f32).reshape(1, D),
        "b1r_d": np.asarray(inputs["L0_ffn_b1"], f32).reshape(1, FF),
        "b2r_d": np.asarray(inputs["L0_ffn_b2"], f32).reshape(1, D),
        "cb_d": asc(np.broadcast_to(cbv, (128, NCB))),
        "idb_d": np.eye(128, dtype=f32).astype(BFNP),
        "idf_d": np.eye(128, dtype=f32),
        "e16_d": e16,
        "onesf_d": np.ones((1, 512), f32),
        "onesb_d": np.ones((128, 1), f32).astype(BFNP),
    }
    x = asc(np.asarray(inputs["L0_attn_x"], f32))
    in_maps = []
    for i in range(NCORES):
        m = dict(common)
        m["xin"] = x[i * BCORE:(i + 1) * BCORE]
        in_maps.append(m)
    return in_maps


def kernel(**inputs):
    in_maps = _host_inputs(inputs)
    nc = build_program()
    res = run_bass_kernel_spmd(nc, in_maps, core_ids=list(range(NCORES)))
    return np.concatenate([np.asarray(r["yout"]) for r in res.results], axis=0)




# version 23 (final)
# speedup vs baseline: 1.0153x; 1.0153x over previous
import contextlib
import numpy as np
import ml_dtypes
import concourse.bass as bass
import concourse.mybir as mybir
from concourse.bass_utils import run_bass_kernel_spmd

F32 = mybir.dt.float32
BF16 = mybir.dt.bfloat16
AF = mybir.ActivationFunctionType
ALU = mybir.AluOpType
AXX = mybir.AxisListType.X
BFNP = ml_dtypes.bfloat16

B, N, D, H, DK, FF = 32, 2048, 1024, 16, 64, 4096
NCORES = 8
BCORE = B // NCORES
GA = 8
GB = 8

M1 = 12582912.0
M256 = 3221225472.0
CNORM = float(DK ** -0.25)
EPS_LN = 1e-5
EPS_ATT = 1e-6

CB_VALS = [CNORM, 4.0, 4.0 * M1, 2.0 ** -10, -3145728.0, 1.0, -4.0, -3.75,
           -49152.0, -49151.75, -3096580.0, 1e-5, 0.65536, 256.0]
(CB_CN, CB_4, CB_P4M1, CB_P10, CB_NM256S, CB_1, CB_N4, CB_N375,
 CB_NY0, CB_NY1, CB_SAB, CB_EPSLN, CB_EPS4, CB_256) = range(14)
NCB = 16


class Prog:
    ENGS = ("T", "V", "S", "G", "Y")
    SEMS = ("T", "V", "S", "G", "YI0", "YI1", "YS0", "YS1", "YA", "YB",
            "YX", "YX2", "YX3", "YX4", "W0", "W1", "S0", "S1")

    def __init__(self, nc):
        self.nc = nc
        self.sem = {}
        self.count = {e: 0 for e in self.SEMS}
        self.ops = {e: [] for e in self.ENGS}
        self.waited = {e: {} for e in self.ENGS}

    def make_sems(self, es):
        for e in self.SEMS:
            self.sem[e] = es.enter_context(self.nc.semaphore(f"sem{e}"))

    def emit(self, eng, fn, waits=(), inc=None):
        waits = list(waits)
        if eng in ("V", "S", "G"):
            waits.append((eng, self.count[eng]))
            if inc is None:
                inc = eng
        todo = []
        for t in waits:
            if t is None:
                continue
            we, wv = t
            if wv <= 0:
                continue
            assert wv <= self.count[we], f"future token {we}:{wv} (now {self.count[we]})"
            if self.waited[eng].get(we, 0) < wv:
                self.waited[eng][we] = wv
                todo.append((we, wv))
        tok = None
        if inc is True:
            inc = eng
        if inc:
            step = 16 if inc not in ("T", "V", "S", "G") else 1
            self.count[inc] += step
            tok = (inc, self.count[inc])
            sems = self.sem

            def run(h, fn=fn, todo=todo, sems=sems, inc=inc, step=step):
                for (we, v) in todo:
                    h.wait_ge(sems[we], v)
                fn(h).then_inc(sems[inc], step)
        else:
            sems = self.sem

            def run(h, fn=fn, todo=todo, sems=sems):
                for (we, v) in todo:
                    h.wait_ge(sems[we], v)
                fn(h)
        self.ops[eng].append(run)
        return tok

    def tok(self, eng):
        return (eng, self.count[eng])


DBG_NSEQ = BCORE
DBG_PHASE = 3
DBG_PA = 5
DBG_LN = 3


def build_program():
    nc = bass.Bass("TRN2", target_bir_lowering=False)
    dt = nc.dram_tensor

    xin = dt("xin", [BCORE, N, D], F32, kind="ExternalInput").ap()
    wq_t = dt("wq_t", [D, D], F32, kind="ExternalInput").ap()
    wk_t = dt("wk_t", [D, D], F32, kind="ExternalInput").ap()
    wv_t = dt("wv_t", [D, D], F32, kind="ExternalInput").ap()
    wo_t = dt("wo_t", [D, D], F32, kind="ExternalInput").ap()
    w1_d = dt("w1_d", [D, FF], F32, kind="ExternalInput").ap()
    w2_d = dt("w2_d", [FF, D], F32, kind="ExternalInput").ap()
    pbd_d = dt("pbd_d", [128, 128], F32, kind="ExternalInput").ap()
    bql_d = dt("bql_d", [128, 8], F32, kind="ExternalInput").ap()
    b1l_d = dt("b1l_d", [128, 32], F32, kind="ExternalInput").ap()
    bkl_d = dt("bkl_d", [128, 8], F32, kind="ExternalInput").ap()
    bvr_d = dt("bvr_d", [1, D], F32, kind="ExternalInput").ap()
    bor_d = dt("bor_d", [1, D], F32, kind="ExternalInput").ap()
    b1r_d = dt("b1r_d", [1, FF], F32, kind="ExternalInput").ap()
    b2r_d = dt("b2r_d", [1, D], F32, kind="ExternalInput").ap()
    cb_d = dt("cb_d", [128, NCB], F32, kind="ExternalInput").ap()
    idb_d = dt("idb_d", [128, 128], BF16, kind="ExternalInput").ap()
    idf_d = dt("idf_d", [128, 128], F32, kind="ExternalInput").ap()
    e16_d = dt("e16_d", [16, D], F32, kind="ExternalInput").ap()
    onesf_d = dt("onesf_d", [1, 512], F32, kind="ExternalInput").ap()
    onesb_d = dt("onesb_d", [128, 1], BF16, kind="ExternalInput").ap()
    yout = dt("yout", [BCORE, N, D], F32, kind="ExternalOutput").ap()
    wsc_q = dt("wsc_q", [128, 8 * D], BF16).ap()
    wsc_k = dt("wsc_k", [128, 8 * D], BF16).ap()
    wsc_v = dt("wsc_v", [128, 8 * D], BF16).ap()
    wsc_o = dt("wsc_o", [128, 8 * D], BF16).ap()
    wsc_2 = dt("wsc_2", [128, 32 * D], BF16).ap()

    es = contextlib.ExitStack()
    with es:
        sbt = lambda name, shape, dtp: es.enter_context(nc.sbuf_tensor(name, shape, dtp))
        pst_ = lambda name, shape, dtp: es.enter_context(nc.psum_tensor(name, shape, dtp))

        w1sb = sbt("w1sb", [128, 8 * FF], BF16)
        wa0 = sbt("wa0", [128, 8 * D], BF16)
        wa1 = sbt("wa1", [128, 8 * D], BF16)
        w2b0 = sbt("w2b0", [128, 2 * D], BF16)
        w2b1 = sbt("w2b1", [128, 2 * D], BF16)
        xbuf = sbt("xbuf", [128, 2048], F32)
        x2b = sbt("x2b", [128, 2048], F32)
        lnf = sbt("lnf", [128, 4096], BF16)
        poolB = sbt("poolB", [128, 10240], BF16)
        poolF = sbt("poolF", [128, 8080], F32)
        kvz = sbt("kvz", [128, 1024], BF16)
        ksz = sbt("ksz", [128, 16], BF16)
        lnstats = sbt("lnstats", [128, 32], F32)
        cb = sbt("cb", [128, NCB], F32)
        idb = sbt("idb", [128, 128], BF16)
        idf = sbt("idf", [128, 128], F32)
        shrow = sbt("shrow", [128, 5376], F32)
        e16 = shrow[0:16, 4352:4352 + D]
        ones0 = shrow[0:1, 4096:4352]
        onesf = shrow[32:33, 0:512]
        onesb = sbt("onesb", [128, 1], BF16)
        pbd = sbt("pbd", [128, 128], BF16)
        bql = sbt("bql", [128, 8], F32)
        bkl = sbt("bkl", [128, 8], F32)
        b1l = sbt("b1l", [128, 32], F32)
        b1l2 = sbt("b1l2", [128, 32], F32)
        cbq = sbt("cbq", [128, 8], F32)
        cbk = sbt("cbk", [128, 8], F32)
        bvr = shrow[32:33, 512:512 + D]
        bor = shrow[32:33, 1536:1536 + D]
        b1r = shrow[0:1, 0:FF]
        b2r = shrow[32:33, 2560:2560 + D]

        LN1TM = 0
        KFM = 2048
        KFTM = 4096
        VTM = 6144
        QFM = 2048
        QFFM = 4096
        AFM = 6144
        YQ = 8192
        CONVT = 0

        CH = [tuple(pb * 3072 + i * 512 for i in range(6)) for pb in range(2)]
        REX = 6144
        SQS = 6656
        LNT = 6656
        XI2 = 0
        DEN = 7680
        RDEN = 7744
        RD16 = 7808
        ST0 = 8064
        S_S1, S_S2A, S_S2B, S_M, S_MS, S_VAR, S_SD, S_R, S_NMR = range(9)

        def pf(slot, width=256):
            return poolF[:, slot:slot + width]

        def stat(i):
            return poolF[:, ST0 + i:ST0 + i + 1]

        psA = pst_("psA", [128, 512], F32)
        psB = pst_("psB", [128, 512], F32)
        psC = pst_("psC", [128, 512], F32)
        psD = pst_("psD", [128, 512], F32)
        psE = pst_("psE", [128, 512], F32)
        psF = pst_("psF", [128, 512], F32)
        psG = pst_("psG", [128, 512], F32)
        psT = pst_("psT", [128, 1024], BF16)

        p = Prog(nc)
        p.make_sems(es)
        state = {"sq_guard": None, "psT_guard": [], "tp_eng": True,
                 "sqs_guard": []}

        def ln_stats(srcs, eps_ap, scale_ap, bias_ap, wait_first=()):
            wv = list(wait_first)
            ntot = sum(s.shape[-1] for s in srcs)
            rtoks = []
            for i, s in enumerate(srcs):
                if i == 0:
                    rtoks.append(p.emit("V", lambda v, s=s: v.reduce_sum(
                        out=stat(S_S1), in_=s, axis=AXX), waits=wv, inc=True))
                else:
                    rtoks.append(p.emit("V", lambda v, s=s: v.reduce_sum(
                        out=stat(S_M), in_=s, axis=AXX), inc=True))
                    p.emit("V", lambda v: v.tensor_tensor(out=stat(S_S1), in0=stat(S_S1),
                                                          in1=stat(S_M), op=ALU.add))
            stoks = []
            for i, s in enumerate(srcs):
                acc = stat(S_S2A) if i == 0 else stat(S_S2B)
                gw = wv + ([state["sq_guard"]] if state["sq_guard"] else [])
                gw = gw + list(state["sqs_guard"])
                gw = gw + [rtoks[i]]
                stoks.append(p.emit("S", lambda s_, s=s, acc=acc: s_.activation(
                    out=poolF[:, SQS:SQS + s.shape[-1]], in_=s, func=AF.Square,
                    accum_out=acc), waits=gw, inc=True))
                if i > 0:
                    p.emit("V", lambda v: v.tensor_tensor(out=stat(S_S2A), in0=stat(S_S2A),
                                                          in1=stat(S_S2B), op=ALU.add),
                           waits=[stoks[-1]])
            vtok = p.emit("V", lambda v, ntot=ntot: v.tensor_scalar(
                out=poolF[:, ST0 + S_M:ST0 + S_MS + 1],
                in0=poolF[:, ST0 + S_S1:ST0 + S_S2A + 1],
                scalar1=1.0 / ntot, scalar2=None,
                op0=ALU.mult), waits=[stoks[-1]], inc=True)
            state["sq_guard"] = vtok
            p.emit("V", lambda v: v.tensor_tensor(out=stat(S_VAR), in0=stat(S_M),
                                                  in1=stat(S_M), op=ALU.mult))
            vtok2 = p.emit("V", lambda v: v.scalar_tensor_tensor(
                out=stat(S_VAR), in0=stat(S_VAR), scalar=-1.0, in1=stat(S_MS),
                op0=ALU.mult, op1=ALU.add), inc=True)
            sdt = p.emit("S", lambda s_, eps_ap=eps_ap: s_.activation(
                out=stat(S_SD), in_=stat(S_VAR), func=AF.Sqrt,
                scale=cb[:, CB_1:CB_1 + 1], bias=eps_ap), waits=[vtok2], inc=True)
            p.emit("V", lambda v, scale_ap=scale_ap: v.reciprocal(
                out=scale_ap, in_=stat(S_SD)), waits=[sdt])
            rtok = p.emit("V", lambda v, scale_ap=scale_ap, bias_ap=bias_ap: v.scalar_tensor_tensor(
                out=bias_ap, in0=stat(S_M), scalar=-1.0, in1=scale_ap,
                op0=ALU.mult, op1=ALU.mult), inc=True)
            return vtok, rtok

        def transpose_tile(src_slot, dst_base, dst_stride, t_idx, ntok):
            ttok = None
            for dc in range(8):
                ttok = p.emit("T", lambda tt, src_slot=src_slot, dc=dc: tt.transpose(
                    psT[:, dc * 128:(dc + 1) * 128],
                    in_=poolB[:, src_slot + dc * 128:src_slot + dc * 128 + 128],
                    identity=idb[:]),
                    waits=([ntok] + state["psT_guard"]) if dc == 0 else (),
                    inc=(dc == 7))
            eng = "V" if state["tp_eng"] else "S"
            state["tp_eng"] = not state["tp_eng"]
            assert dst_stride == 256
            base = dst_base + t_idx * 128
            full = lnf.rearrange("p (c j) -> p c j", j=256)
            c0, off = base // 256, base % 256
            dst = full[:, c0:c0 + 8, off:off + 128]
            src = psT[:].rearrange("p (c j) -> p c j", j=128)
            fn = ((lambda h, dst=dst, src=src: h.tensor_copy(out=dst, in_=src))
                  if eng == "V" else
                  (lambda h, dst=dst, src=src: h.copy(out=dst, in_=src)))
            tv = p.emit(eng, fn, waits=[ttok], inc=True)
            evs = [tv]
            state["psT_guard"] = evs
            return evs

        for (dst, src) in ((cb[:], cb_d), (idb[:], idb_d), (idf[:], idf_d),
                           (e16, e16_d), (onesf, onesf_d), (onesb[:], onesb_d),
                           (poolF[:, 0:128], pbd_d), (bql[:], bql_d), (bkl[:], bkl_d), (b1l[:], b1l_d),
                           (bvr, bvr_d), (bor, bor_d), (b1r, b1r_d), (b2r, b2r_d),
                           (ones0, onesf_d[0:1, 0:256])):
            p.emit("Y", lambda y, dst=dst, src=src: y.dma_start(out=dst, in_=src[:]),
                   inc="YI0")
        tok_small = p.tok("YI0")

        p.emit("V", lambda v: v.tensor_copy(out=pbd[:], in_=poolF[:, 0:128]), waits=[tok_small])
        p.emit("V", lambda v: v.tensor_scalar(out=b1l2[:], in0=b1l[:], scalar1=256.0,
                                              scalar2=None, op0=ALU.mult))
        p.emit("V", lambda v: v.tensor_scalar(out=cbq[:], in0=bql[:], scalar1=CNORM,
                                              scalar2=None, op0=ALU.mult))
        p.emit("V", lambda v: v.tensor_scalar(out=cbk[:], in0=bkl[:], scalar1=CNORM,
                                              scalar2=None, op0=ALU.mult), inc=True)
        tok_initv = p.tok("V")

        conv_jobs = []
        for (src, dstsc) in ((wq_t, wsc_q), (wk_t, wsc_k), (wv_t, wsc_v), (wo_t, wsc_o)):
            for dc in range(8):
                conv_jobs.append((src[dc * 128:(dc + 1) * 128, :], D,
                                  dstsc[:, dc * D:(dc + 1) * D], 1.0))
        for fc in range(32):
            conv_jobs.append((w2_d[fc * 128:(fc + 1) * 128, :], D,
                              wsc_2[:, fc * D:(fc + 1) * D], 1.0 / 256.0))
        prev_cast = {}
        prev_store = {}
        for i, (src, cols, dstsc, scale) in enumerate(conv_jobs):
            slot = (i % 2) * 1024
            eng = "Y" if i % 2 == 0 else "G"
            ltok = p.emit(eng, lambda h, src=src, slot=slot, cols=cols:
                          h.dma_start(out=poolF[:, slot:slot + cols], in_=src),
                          waits=[prev_cast.get(i - 2), tok_initv], inc="YI%d" % (i % 2))
            ctok = p.emit("V", lambda v, slot=slot, cols=cols, scale=scale:
                          v.tensor_scalar(out=poolB[:, CONVT + slot:CONVT + slot + cols],
                                          in0=poolF[:, slot:slot + cols],
                                          scalar1=scale, scalar2=None, op0=ALU.mult),
                          waits=[ltok, prev_store.get(i - 2)], inc=True)
            prev_cast[i] = ctok
            p.emit(eng, lambda h, slot=slot, cols=cols, dstsc=dstsc:
                   h.dma_start(out=dstsc, in_=poolB[:, CONVT + slot:CONVT + slot + cols]),
                   waits=[ctok], inc="YS%d" % (i % 2))
            prev_store[i] = p.tok("YS%d" % (i % 2))
        tok_scratch = (p.tok("YS0"), p.tok("YS1"))
        lastw1 = prev_cast[len(conv_jobs) - 1]
        for i in range(8):
            src = w1_d[i * 128:(i + 1) * 128, :]
            p.emit("Y", lambda y, src=src: y.dma_start(
                out=poolF[:, 0:FF // 2], in_=src[:, 0:FF // 2]),
                waits=[lastw1], inc="YI0")
            lt0 = p.tok("YI0")
            p.emit("G", lambda g_, src=src: g_.dma_start(
                out=poolF[:, FF // 2:FF], in_=src[:, FF // 2:FF]),
                waits=[lastw1], inc="YI1")
            lt1 = p.tok("YI1")
            lastw1 = p.emit("V", lambda v, i=i: v.tensor_copy(
                out=w1sb[:, i * FF:(i + 1) * FF], in_=poolF[:, 0:FF]),
                waits=[lt0, lt1], inc=True)
        tok_w1 = lastw1

        pending_final = None
        for b in range(min(BCORE, DBG_NSEQ) if DBG_PHASE >= 2 else 0):
            if pending_final is not None:
                fin_seq, _frest = pending_final()
                _frest()
                pending_final = None
            else:
                fin_seq = []
            kv_first = True
            vt_prev = ()
            kf_prev = None
            kt_hist = {}
            sub_idx = 0
            last_kv_tok = None

            PA_LNB = (LN1TM, 8192)
            PA_LNF = (0, 2048)
            pa_norm = {}
            pa_tr_tok = {}
            pa_lnfd = {}
            pa_xtok = {}

            def pa_xload(g, waits):
                xa = xbuf if g % 2 == 0 else x2b
                cls0 = "YX" if g % 2 == 0 else "YX3"
                cls1 = "YX2" if g % 2 == 0 else "YX4"
                p.emit("Y", lambda y, b=b, g=g, xa=xa: y.dma_start(
                    out=xa[:, 0:1024], in_=xin[b, g * 256:g * 256 + 128, :]),
                    waits=waits, inc=cls0)
                ta = p.tok(cls0)
                p.emit("G", lambda g_, b=b, g=g, xa=xa: g_.dma_start(
                    out=xa[:, 1024:2048], in_=xin[b, g * 256 + 128:g * 256 + 256, :]),
                    waits=waits, inc=cls1)
                tb = p.tok(cls1)
                pa_xtok[g] = (ta, tb)

            def pa_lnchain(g):
                xa = xbuf if g % 2 == 0 else x2b
                ta, tb = pa_xtok[g]
                ntoks = []
                for t in range(2):
                    xt = xa[:, t * 1024:(t + 1) * 1024]
                    kidx = 2 * (g * 2 + t)
                    _, rtok = ln_stats([xt], cb[:, CB_EPSLN:CB_EPSLN + 1],
                                       lnstats[:, kidx:kidx + 1],
                                       lnstats[:, kidx + 1:kidx + 2],
                                       wait_first=[ta if t == 0 else tb])
                    slotB = PA_LNB[g % 2] + t * 1024
                    ws = list(tok_scratch) if g <= 1 else []
                    ntok = p.emit("S", lambda s, xt=xt, kidx=kidx, slotB=slotB: s.activation(
                        out=poolB[:, slotB:slotB + 1024], in_=xt, func=AF.Identity,
                        bias=lnstats[:, kidx + 1:kidx + 2],
                        scale=lnstats[:, kidx:kidx + 1]),
                        waits=[rtok, pa_tr_tok.get(g - 2)] + ws, inc=True)
                    ntoks.append(ntok)
                pa_norm[g] = ntoks

            def pa_transposes(g):
                evs = []
                for t in range(2):
                    slotB = PA_LNB[g % 2] + t * 1024
                    evs.extend(transpose_tile(slotB, PA_LNF[g % 2], 256, t,
                                              pa_norm[g][t]))
                pa_tr_tok[g] = p.tok("T")
                pa_lnfd[g] = list(evs)

            pa_xload(0, [p.tok("V"), p.tok("S"), p.tok("G")])
            pa_xload(1, [p.tok("V"), p.tok("S"), p.tok("G")])
            tok_wa0 = p.emit("Y", lambda y: y.dma_start(out=wa0[:], in_=wsc_k[:]),
                             waits=[*tok_scratch, p.tok("T")], inc="YA")
            tok_wa1 = p.emit("G", lambda g_: g_.dma_start(out=wa1[:], in_=wsc_v[:]),
                             waits=[*tok_scratch, p.tok("T")], inc="YB")
            pa_lnchain(0)
            pa_transposes(0)

            for g in range(GA):
                if g + 2 < GA:
                    pa_xload(g + 2, pa_norm[g])
                lnf_base = PA_LNF[g % 2]
                lnf_done = pa_lnfd[g]

                if DBG_PA >= 2:
                    kevac = {}
                    for c in range(8):
                        bank = psA if c % 2 == 0 else psB
                        wv = list(lnf_done) + [tok_wa0, kevac.get(c - 2)] + fin_seq
                        mt = None
                        for dc in range(8):
                            mt = p.emit("T", lambda tt, bank=bank, c=c, dc=dc,
                                        lnf_base=lnf_base: tt.matmul(
                                bank[:, 0:256],
                                lhsT=wa0[:, dc * D + c * 128:dc * D + (c + 1) * 128],
                                rhs=lnf[:, lnf_base + dc * 256:lnf_base + (dc + 1) * 256],
                                start=(dc == 0), stop=(dc == 7)),
                                waits=wv if dc == 0 else (), inc=(dc == 7))
                        kevac[c] = p.emit("S", lambda s, bank=bank, c=c: s.activation(
                            out=poolB[:, KFM + c * 256:KFM + (c + 1) * 256], in_=bank[:, 0:256],
                            func=AF.Identity, scale=cb[:, CB_CN:CB_CN + 1], bias=cbk[:, c:c + 1]),
                            waits=[mt], inc=True)
                    kfm_done = [kevac[6], kevac[7]]

                if g + 1 < GA:
                    pa_lnchain(g + 1)

                for s in (range(2) if DBG_PA >= 3 else []):
                    kslot = KFTM + (s % 2) * 1024
                    wv = list(kfm_done) + [kf_prev]
                    ft = None
                    for gH in range(8):
                        bank = psA if gH < 4 else psB
                        col = (gH % 4) * 128
                        ft = p.emit("T", lambda tt, bank=bank, col=col, gH=gH, s=s: tt.matmul(
                            bank[:, col:col + 128],
                            lhsT=poolB[:, KFM + gH * 256 + s * 128:KFM + gH * 256 + s * 128 + 128],
                            rhs=pbd[:], start=True, stop=True),
                            waits=wv if gH == 0 else (), inc=(gH == 7))

                    vslot = VTM + (s % 2) * 1024
                    wv = list(lnf_done) + [tok_wa1, *vt_prev] + fin_seq
                    mt = None
                    for half, bank in ((0, psC), (1, psD)):
                        for dc in range(8):
                            mt = p.emit("T", lambda tt, bank=bank, s=s, dc=dc, half=half,
                                        lnf_base=lnf_base: tt.matmul(
                                bank[:],
                                lhsT=lnf[:, lnf_base + dc * 256 + s * 128:
                                         lnf_base + dc * 256 + s * 128 + 128],
                                rhs=wa1[:, dc * D + half * 512:dc * D + half * 512 + 512],
                                start=(dc == 0), stop=False),
                                waits=wv if (dc == 0 and half == 0) else (), inc=False)
                        mt = p.emit("T", lambda tt, bank=bank, half=half: tt.matmul(
                            bank[:], lhsT=onesf[:, 0:128],
                            rhs=bvr[:, half * 512:half * 512 + 512],
                            start=False, stop=True), inc=(half == 1))
                    vtv = p.emit("V", lambda v, vslot=vslot: v.tensor_copy(
                        out=poolB[:, vslot:vslot + 512], in_=psC[:]),
                        waits=[mt, kt_hist.get(sub_idx - 2)], inc=True)
                    vts = p.emit("S", lambda s_, vslot=vslot: s_.copy(
                        out=poolB[:, vslot + 512:vslot + 1024], in_=psD[:]),
                        waits=[mt, kt_hist.get(sub_idx - 2)], inc=True)
                    vt = (vtv, vts)
                    kf = None
                    for gH in range(8):
                        bank = psA if gH < 4 else psB
                        col = (gH % 4) * 128
                        kf = p.emit("V", lambda v, bank=bank, col=col, kslot=kslot, gH=gH: v.tensor_scalar(
                            out=poolB[:, kslot + gH * 128:kslot + gH * 128 + 128],
                            in0=bank[:, col:col + 128], scalar1=EPS_ATT, scalar2=EPS_ATT,
                            op0=ALU.add, op1=ALU.max),
                            waits=[ft, kt_hist.get(sub_idx - 2)] if gH == 0 else [ft],
                            inc=(gH == 7))
                    vt_prev, kf_prev = vt, kf

                    if DBG_PA < 4:
                        continue
                    last = (g == GA - 1 and s == 1)
                    kt = None
                    for gH in range(8):
                        bank = psF if gH < 4 else psG
                        col = (gH % 4) * 128
                        st_ = kv_first and (gH % 4 == 0)
                        sp_ = last and (gH % 4 == 3)
                        p.emit("T", lambda tt, bank=bank, col=col, kslot=kslot, vslot=vslot,
                               gH=gH, st_=st_, sp_=sp_: tt.matmul(
                            bank[:, col:col + 128],
                            lhsT=poolB[:, kslot + gH * 128:kslot + gH * 128 + 128],
                            rhs=poolB[:, vslot + gH * 128:vslot + gH * 128 + 128],
                            start=st_, stop=sp_),
                            waits=[kf, *vt] if gH == 0 else (), inc=False)
                        kt = p.emit("T", lambda tt, kslot=kslot, gH=gH,
                                    st_=(kv_first and gH == 0),
                                    sp_=(last and gH == 7): tt.matmul(
                            psE[:, gH:gH + 1],
                            lhsT=poolB[:, kslot + gH * 128:kslot + gH * 128 + 128],
                            rhs=onesb[:], start=st_, stop=sp_),
                            inc=(gH == 7))
                    kv_first = False
                    kt_hist[sub_idx] = kt
                    sub_idx += 1
                    last_kv_tok = kt

                if g + 1 < GA:
                    pa_transposes(g + 1)

            if DBG_PA < 5:
                continue
            p.emit("V", lambda v: v.memset(kvz[:], 0.0), waits=[last_kv_tok])
            p.emit("V", lambda v: v.memset(ksz[:], 0.0))
            kv_ready = None
            for gH in range(8):
                bank = psF if gH < 4 else psG
                col = (gH % 4) * 128
                p.emit("V", lambda v, bank=bank, col=col, gH=gH: v.tensor_copy(
                    out=kvz[0:64, gH * 128:gH * 128 + 64], in_=bank[0:64, col:col + 64]))
                p.emit("V", lambda v, bank=bank, col=col, gH=gH: v.tensor_copy(
                    out=kvz[64:128, gH * 128 + 64:gH * 128 + 128],
                    in_=bank[64:128, col + 64:col + 128]))
                p.emit("V", lambda v, gH=gH: v.tensor_copy(
                    out=ksz[0:64, 2 * gH:2 * gH + 1], in_=psE[0:64, gH:gH + 1]))
                kv_ready = p.emit("V", lambda v, gH=gH: v.tensor_copy(
                    out=ksz[64:128, 2 * gH + 1:2 * gH + 2], in_=psE[64:128, gH:gH + 1]),
                    inc=(gH == 7))

            if DBG_PHASE < 3:
                continue
            tok_wq = p.emit("Y", lambda y: y.dma_start(out=wa0[:], in_=wsc_q[:]),
                            waits=[*tok_scratch, p.tok("T")], inc="YA")
            tok_wo = p.emit("G", lambda g_: g_.dma_start(out=wa1[:], in_=wsc_o[:]),
                            waits=[*tok_scratch, p.tok("T")], inc="YB")

            w2_cons = {0: None, 1: None}
            store_tok = {}
            prev_xbuf_toks = []
            fin_prev = []
            prev_h_ev = []
            pre_prologue = None

            def emit_prologue_x(g8):
                xw = ([p.tok("V"), p.tok("S"), p.tok("G")] if g8 == 0 else
                      list(prev_xbuf_toks))
                p.emit("Y", lambda y, b=b, g8=g8: y.dma_start(
                    out=xbuf[:, 0:1024], in_=xin[b, g8 * 256:g8 * 256 + 128, :]),
                    waits=xw, inc="YX")
                txa = p.tok("YX")
                p.emit("G", lambda g_, b=b, g8=g8: g_.dma_start(
                    out=xbuf[:, 1024:2048], in_=xin[b, g8 * 256 + 128:g8 * 256 + 256, :]),
                    waits=xw, inc="YX2")
                txb = p.tok("YX2")
                return txa, txb

            def emit_prologue_norm(g8, txa, txb):
                pbg = list(state["psT_guard"])
                ln1_ntoks = []
                for t in range(2):
                    xt = xbuf[:, t * 1024:(t + 1) * 1024]
                    kidx = 2 * (g8 * 2 + t)
                    slotB = LN1TM + t * 1024
                    ntok = p.emit("S", lambda s, xt=xt, kidx=kidx, slotB=slotB: s.activation(
                        out=poolB[:, slotB:slotB + 1024], in_=xt, func=AF.Identity,
                        bias=lnstats[:, kidx + 1:kidx + 2],
                        scale=lnstats[:, kidx:kidx + 1]),
                        waits=[txa if t == 0 else txb, kv_ready] + pbg, inc=True)
                    ln1_ntoks.append(ntok)
                return ln1_ntoks

            def emit_prologue_tr(ln1_ntoks):
                lnf_done = []
                for t in range(2):
                    slotB = LN1TM + t * 1024
                    lnf_done = transpose_tile(slotB, 0, 256, t, ln1_ntoks[t])
                return lnf_done

            def make_att(lnfd, rd_prev):
                st = {"qev": {}, "qf_ev": {}, "rd_tok": None, "dt": None}

                def q_stage(c):
                    off = (c % 2) * 256
                    wv = list(lnfd) + [tok_wq, st["qev"].get(c - 1)]
                    if c == 0:
                        wv.append(rd_prev)
                    mt = None
                    for dc in range(8):
                        mt = p.emit("T", lambda tt, off=off, c=c, dc=dc: tt.matmul(
                            psG[:, off:off + 256],
                            lhsT=wa0[:, dc * D + c * 128:dc * D + (c + 1) * 128],
                            rhs=lnf[:, dc * 256:(dc + 1) * 256],
                            start=(dc == 0), stop=(dc == 7)),
                            waits=wv if dc == 0 else (), inc=(dc == 7))
                    st["qev"][c] = p.emit("S", lambda s, off=off, c=c: s.activation(
                        out=poolB[:, QFM + c * 256:QFM + (c + 1) * 256],
                        in_=psG[:, off:off + 256],
                        func=AF.Identity, scale=cb[:, CB_CN:CB_CN + 1],
                        bias=cbq[:, c:c + 1]), waits=[mt], inc=True)

                def qf_stage(gH):
                    off = (gH % 2) * 256
                    ft = p.emit("T", lambda tt, off=off, gH=gH: tt.matmul(
                        psG[:, off:off + 256], lhsT=pbd[:],
                        rhs=poolB[:, QFM + gH * 256:QFM + (gH + 1) * 256],
                        start=True, stop=True),
                        waits=[st["qev"][gH], st["qf_ev"].get(gH - 1)]
                        + ([st["qev"][7]] if gH == 0 else []), inc=True)
                    st["qf_ev"][gH] = p.emit("V", lambda v, off=off, gH=gH: v.tensor_scalar(
                        out=poolB[:, QFFM + gH * 256:QFFM + (gH + 1) * 256],
                        in0=psG[:, off:off + 256], scalar1=EPS_ATT, scalar2=EPS_ATT,
                        op0=ALU.add, op1=ALU.max),
                        waits=[ft] + ([st["qev"][7]] if gH == 0 else []), inc=True)

                def den_stage():
                    dt_ = None
                    for s in range(2):
                        for gH in range(8):
                            wv = [st["qf_ev"][gH]]
                            if s == 0 and gH == 0:
                                wv += [kv_ready, st["qf_ev"][7]]
                            dt_ = p.emit("T", lambda tt, s=s, gH=gH: tt.matmul(
                                psG[:, s * 16 + 2 * gH:s * 16 + 2 * gH + 2],
                                lhsT=poolB[:, QFFM + gH * 256 + s * 128:
                                           QFFM + gH * 256 + s * 128 + 128],
                                rhs=ksz[:, 2 * gH:2 * gH + 2],
                                start=(s == 0 and gH == 0),
                                stop=(s == 1 and gH == 7)),
                                waits=wv, inc=(s == 1 and gH == 7))
                    st["dt"] = dt_

                def rden_stage():
                    p.emit("V", lambda v: v.tensor_scalar(
                        out=pf(DEN, 32), in0=psG[:, 0:32],
                        scalar1=EPS_ATT, scalar2=None, op0=ALU.add),
                        waits=[st["dt"]])
                    den_tok = p.emit("V", lambda v: v.reciprocal(
                        out=pf(RDEN, 32), in_=pf(DEN, 32)), inc=True)
                    rt = None
                    for s in range(2):
                        rt = p.emit("T", lambda tt, s=s: tt.transpose(
                            psG[0:16, 64 + s * 128:64 + (s + 1) * 128],
                            in_=poolF[:, RDEN + s * 16:RDEN + (s + 1) * 16],
                            identity=idf[:]), waits=[den_tok] if s == 0 else (),
                            inc=(s == 1))
                    st["rd_tok"] = p.emit("V", lambda v: v.tensor_copy(
                        out=poolF[0:16, RD16:RD16 + 256], in_=psG[0:16, 64:320]),
                        waits=[rt], inc=True)

                def rex_stage(gH, guard=()):
                    bankR = psE if gH % 2 == 0 else psF
                    st["rex_mt"][gH] = p.emit("T", lambda tt, bankR=bankR, gH=gH: tt.matmul(
                        bankR[:, 0:256], lhsT=e16[:, gH * 128:(gH + 1) * 128],
                        rhs=poolF[0:16, RD16:RD16 + 256], start=True, stop=True),
                        waits=[st["rd_tok"]] + list(guard), inc=True)

                st["rex_mt"] = {}
                st["rex_fn"] = rex_stage
                return st, q_stage, qf_stage, den_stage, rden_stage, rex_stage

            att_pre = None
            att_stage_fns = None

            for g8 in range(GB):
                tk0 = g8 * 2
                w2tok = {}
                for e8 in range(2):
                    dst = w2b0 if e8 % 2 == 0 else w2b1
                    w2tok[e8] = p.emit("Y", lambda y, dst=dst, e8=e8: y.dma_start(
                        out=dst[:], in_=wsc_2[:, e8 * 2 * D:(e8 + 1) * 2 * D]),
                        waits=[*tok_scratch, w2_cons[e8 % 2]], inc="W%d" % (e8 % 2))

                if pre_prologue is not None:
                    ln1_ntoks, lnf_done = pre_prologue
                    pre_prologue = None
                else:
                    txa, txb = emit_prologue_x(g8)
                    ln1_ntoks = emit_prologue_norm(g8, txa, txb)
                    lnf_done = emit_prologue_tr(ln1_ntoks)

                if att_pre is not None:
                    att = att_pre
                    att_pre = None
                else:
                    att, q_s, qf_s, den_s, rden_s, rex_s0 = make_att(lnf_done, kv_ready)
                    for c in range(8):
                        q_s(c)
                    for gH in range(8):
                        qf_s(gH)
                    den_s()
                    rden_s()
                qf_ev = att["qf_ev"]
                rd_tok = att["rd_tok"]

                fin_rest = None
                if pending_final is not None:
                    fin_prev, fin_rest = pending_final()
                    pending_final = None

                a_ev = {}
                rex_s = att["rex_fn"]
                for gH in range(8):
                    rex_s(gH, [a_ev.get(gH - 2)]
                          + (prev_h_ev if gH <= 1 else []))
                    bankR = psE if gH % 2 == 0 else psF
                    bankA = psC if gH % 2 == 0 else psD
                    rslot = REX + (gH % 2) * 256
                    rc = p.emit("V" if gH % 2 == 0 else "S",
                                (lambda v, bankR=bankR, rslot=rslot: v.tensor_copy(
                                    out=pf(rslot, 256), in_=bankR[:, 0:256]))
                                if gH % 2 == 0 else
                                (lambda s_, bankR=bankR, rslot=rslot: s_.copy(
                                    out=pf(rslot, 256), in_=bankR[:, 0:256])),
                        waits=[att["rex_mt"][gH], a_ev.get(gH - 2)], inc=True)
                    at = p.emit("T", lambda tt, bankA=bankA, gH=gH: tt.matmul(
                        bankA[:, 0:256], lhsT=kvz[:, gH * 128:(gH + 1) * 128],
                        rhs=poolB[:, QFFM + gH * 256:QFFM + (gH + 1) * 256],
                        start=True, stop=True),
                        waits=[qf_ev[gH], kv_ready, a_ev.get(gH - 2),
                               qf_ev.get(7) if gH <= 1 else None]
                        + (fin_prev[:1] if gH <= 1 else []), inc=True)
                    a_ev[gH] = p.emit("V", lambda v, bankA=bankA, rslot=rslot, gH=gH: v.tensor_tensor(
                        out=poolB[:, AFM + gH * 256:AFM + (gH + 1) * 256],
                        in0=bankA[:, 0:256], in1=pf(rslot, 256), op=ALU.mult),
                        waits=[at, rc], inc=True)

                if fin_rest is not None:
                    fin_rest()

                ffn_guard = []
                fg_s = {}
                x2_toks = []
                wo_mt = {}
                for s in range(2):
                    b0 = psA if s == 0 else psC
                    b1_ = psB if s == 0 else psD
                    wv = [a_ev[7], tok_wo] + fin_prev
                    mt = None
                    for half, bank in ((0, b0), (1, b1_)):
                        for dc in range(8):
                            mt = p.emit("T", lambda tt, bank=bank, s=s, dc=dc, half=half: tt.matmul(
                                bank[:],
                                lhsT=poolB[:, AFM + dc * 256 + s * 128:AFM + dc * 256 + s * 128 + 128],
                                rhs=wa1[:, dc * D + half * 512:dc * D + half * 512 + 512],
                                start=(dc == 0), stop=False),
                                waits=wv if (dc == 0 and half == 0) else (), inc=False)
                        mt = p.emit("T", lambda tt, bank=bank, half=half: tt.matmul(
                            bank[:], lhsT=onesf[:, 0:128],
                            rhs=bor[:, half * 512:half * 512 + 512],
                            start=False, stop=True), inc=(half == 1))
                    wo_mt[s] = mt
                for s in range(2):
                    b0 = psA if s == 0 else psC
                    b1_ = psB if s == 0 else psD
                    _, rtok = ln_stats([b0[:], b1_[:]], cb[:, CB_EPSLN:CB_EPSLN + 1], stat(S_R), stat(S_NMR),
                                       wait_first=[wo_mt[s]])
                    n1 = p.emit("S", lambda s_, b0=b0: s_.activation(
                        out=pf(LNT, 512), in_=b0[:], func=AF.Identity,
                        scale=stat(S_R), bias=stat(S_NMR)),
                        waits=[rtok] + list(state["sqs_guard"]), inc=True)
                    n2 = p.emit("S", lambda s_, b1_=b1_: s_.activation(
                        out=poolF[:, LNT + 512:LNT + 1024], in_=b1_[:], func=AF.Identity,
                        scale=stat(S_R), bias=stat(S_NMR)), inc=True)
                    x2t = p.emit("V", lambda v, s=s: v.tensor_tensor(
                        out=x2b[:, s * 1024:(s + 1) * 1024], in0=pf(LNT, 1024),
                        in1=xbuf[:, s * 1024:(s + 1) * 1024], op=ALU.add),
                        waits=[n2], inc=True)
                    state["sqs_guard"] = [x2t]
                    ffn_guard.append(x2t)
                    x2_toks.append(x2t)

                    x2ap = x2b[:, s * 1024:(s + 1) * 1024]
                    _, rtok = ln_stats([x2ap], cb[:, CB_EPSLN:CB_EPSLN + 1], stat(S_R), stat(S_NMR),
                                       wait_first=[x2t])
                    slotB = LN1TM + s * 1024
                    ntok = p.emit("S", lambda s_, x2ap=x2ap, slotB=slotB: s_.activation(
                        out=poolB[:, slotB:slotB + 1024], in_=x2ap, func=AF.Identity,
                        scale=stat(S_R), bias=stat(S_NMR)), waits=[rtok], inc=True)
                    evs = transpose_tile(slotB, 2048, 256, s, ntok)
                    ffn_guard.extend(evs)
                    fg_s[s] = [x2t] + list(evs)
                    if s == 0:
                        for prh in range(2):
                            hbk = psE if prh % 2 == 0 else psF
                            hw = list(fg_s[0]) + [tok_w1, a_ev[7]]
                            for sub in range(2):
                                fc = 2 * prh + sub
                                for dc in range(8):
                                    p.emit("T", lambda tt, hbk=hbk, fc=fc, dc=dc,
                                           sub=sub: tt.matmul(
                                        hbk[:, sub * 256:sub * 256 + 128],
                                        lhsT=w1sb[:, dc * FF + fc * 128:dc * FF + fc * 128 + 128],
                                        rhs=lnf[:, 2048 + dc * 256:2048 + dc * 256 + 128],
                                        start=(sub == 0 and dc == 0), stop=False),
                                        waits=hw if (sub == 0 and dc == 0) else (),
                                        inc=False)

                h_ev = {}
                chain_tok = {}
                po_hist = {}
                po_done = None
                back_p4 = {}

                def emit_back(bpr):
                    SL = CH[bpr % 2]
                    y2t = p.emit("V", lambda v, SL=SL: v.scalar_tensor_tensor(
                        out=pf(SL[3], 512), in0=pf(SL[4], 512), scalar=256.0,
                        in1=pf(SL[1], 512), op0=ALU.mult, op1=ALU.add),
                        waits=[back_p4[bpr]], inc=True)
                    yslot = YQ + (bpr % 4) * 512
                    yqt = p.emit("V", lambda v, yslot=yslot, SL=SL: v.tensor_scalar(
                        out=poolB[:, yslot:yslot + 512], in0=pf(SL[3], 512), scalar1=M1,
                        scalar2=M1, op0=ALU.add, op1=ALU.subtract),
                        waits=[y2t, po_hist.get(bpr - 4)], inc=True)
                    chain_tok[bpr] = yqt
                def emit_po(pr_po):
                    wbuf = w2b0 if pr_po % 2 == 0 else w2b1
                    yslot = YQ + (pr_po % 4) * 512
                    pd = None
                    for sub in range(2):
                        for s2 in range(2):
                            for half in range(2):
                                bank = (psA, psB, psC, psD)[s2 * 2 + half]
                                pd = p.emit("T", lambda tt, bank=bank, wbuf=wbuf,
                                            yslot=yslot, sub=sub, s2=s2, half=half,
                                            pr_po=pr_po: tt.matmul(
                                    bank[:],
                                    lhsT=poolB[:, yslot + sub * 256 + s2 * 128:
                                               yslot + sub * 256 + s2 * 128 + 128],
                                    rhs=wbuf[:, sub * D + half * 512:
                                             sub * D + half * 512 + 512],
                                    start=(pr_po == 0 and sub == 0), stop=False),
                                    waits=[chain_tok[pr_po], w2tok[pr_po]]
                                    if (sub == 0 and s2 == 0 and half == 0) else (),
                                    inc=(sub == 1 and s2 == 1 and half == 1))
                    po_hist[pr_po] = pd
                    w2_cons[pr_po % 2] = pd
                    return pd

                px_next = None
                pro_norm_next = None
                if g8 + 1 < GB:
                    prev_xbuf_toks = x2_toks + ln1_ntoks
                    px_next = emit_prologue_x(g8 + 1)

                def att_late(it):
                    if g8 + 1 >= GB:
                        return
                    if it in (4, 6, 8, 10, 12, 14):
                        att_stage_fns[0]((it - 4) // 2)
                    elif it in (15, 16):
                        att_stage_fns[0](it - 9)
                    elif it in (17, 18):
                        for gH in range(4 * (it - 17), 4 * (it - 16)):
                            att_stage_fns[1](gH)
                    elif it == 19:
                        att_stage_fns[2]()
                        att_stage_fns[3]()

                for it in range(20):
                    if g8 + 1 < GB and it == 2:
                        pro_norm_next = emit_prologue_norm(g8 + 1, *px_next)
                    if g8 + 1 < GB and it == 3:
                        lnfd_next = emit_prologue_tr(pro_norm_next)
                        pre_prologue = (pro_norm_next, lnfd_next)
                        att_n, q_s, qf_s, den_s, rden_s, rex_s = make_att(
                            lnfd_next, att["rd_tok"])
                        att_stage_fns = (q_s, qf_s, den_s, rden_s, rex_s)
                    if it >= 4:
                        po_done = emit_po(it - 4)
                    prl = it - 2
                    if 2 <= prl < 16:
                        dst = w2b0 if prl % 2 == 0 else w2b1
                        w2tok[prl] = p.emit("Y", lambda y, dst=dst, prl=prl: y.dma_start(
                            out=dst[:], in_=wsc_2[:, prl * 2 * D:(prl + 1) * 2 * D]),
                            waits=[w2_cons[prl % 2]], inc="W%d" % (prl % 2))
                    if it >= 16:
                        if it == 16:
                            emit_back(15)
                        att_late(it)
                        if it == 19 and g8 + 1 < GB:
                            att_pre = att_n
                        continue
                    pr = it
                    SL_A, SL_B, SL_C, SL_D, SL_E, SL_F = CH[pr % 2]
                    hbank = psE if pr % 2 == 0 else psF
                    mt = None
                    if pr <= 1:
                        wv = list(fg_s[1]) + [tok_w1]
                        for sub in range(2):
                            fc = 2 * pr + sub
                            for dc in range(8):
                                mt = p.emit("T", lambda tt, hbank=hbank, fc=fc, dc=dc,
                                            sub=sub: tt.matmul(
                                    hbank[:, sub * 256 + 128:sub * 256 + 256],
                                    lhsT=w1sb[:, dc * FF + fc * 128:dc * FF + fc * 128 + 128],
                                    rhs=lnf[:, 2048 + dc * 256 + 128:2048 + (dc + 1) * 256],
                                    start=False,
                                    stop=(sub == 1 and dc == 7)),
                                    waits=wv if (sub == 0 and dc == 0) else (),
                                    inc=(sub == 1 and dc == 7))
                    else:
                        wv = list(ffn_guard) + [tok_w1, h_ev.get(pr - 2)]
                        for sub in range(2):
                            fc = 2 * pr + sub
                            for dc in range(8):
                                mt = p.emit("T", lambda tt, hbank=hbank, fc=fc, dc=dc,
                                            sub=sub: tt.matmul(
                                    hbank[:, sub * 256:sub * 256 + 256],
                                    lhsT=w1sb[:, dc * FF + fc * 128:dc * FF + fc * 128 + 128],
                                    rhs=lnf[:, 2048 + dc * 256:2048 + (dc + 1) * 256],
                                    start=(sub == 0 and dc == 0),
                                    stop=(sub == 1 and dc == 7)),
                                    waits=wv if (sub == 0 and dc == 0) else (),
                                    inc=(sub == 1 and dc == 7))
                    guard = [chain_tok.get(pr - 2)]
                    if pr == 0:
                        guard += [store_tok.get(0), store_tok.get(1)]
                    t1a = p.emit("S", lambda s_, hbank=hbank, SL_A=SL_A, pr=pr: s_.activation(
                        out=pf(SL_A, 256), in_=hbank[:, 0:256], func=AF.Identity,
                        scale=cb[:, CB_256:CB_256 + 1],
                        bias=b1l2[:, 2 * pr:2 * pr + 1]),
                        waits=[mt] + guard, inc=True)
                    t1t = p.emit("S", lambda s_, hbank=hbank, SL_A=SL_A, pr=pr: s_.activation(
                        out=poolF[:, SL_A + 256:SL_A + 512], in_=hbank[:, 256:512],
                        func=AF.Identity, scale=cb[:, CB_256:CB_256 + 1],
                        bias=b1l2[:, 2 * pr + 1:2 * pr + 2]), inc=True)
                    h_ev[pr] = t1t
                    xi4t = p.emit("V", lambda v, SL_A=SL_A, SL_B=SL_B: v.tensor_scalar(
                        out=pf(SL_B, 512), in0=pf(SL_A, 512), scalar1=4.0,
                        scalar2=4.0 * M1, op0=ALU.mult, op1=ALU.add),
                        waits=[t1t] + guard, inc=True)
                    t2t = p.emit("V", lambda v, SL_C=SL_C, SL_B=SL_B: v.tensor_scalar(
                        out=pf(SL_C, 512), in0=pf(SL_B, 512),
                        scalar1=3968.0 - 4.0 * M1, scalar2=M256,
                        op0=ALU.add, op1=ALU.add), waits=[xi4t], inc=True)
                    sat = p.emit("V", lambda v, SL_D=SL_D, SL_C=SL_C: v.tensor_scalar(
                        out=pf(SL_D, 512), in0=pf(SL_C, 512), scalar1=2.0 ** -10,
                        scalar2=-3096580.0, op0=ALU.mult, op1=ALU.add),
                        waits=[t2t], inc=True)
                    y0t = p.emit("S", lambda s_, SL_E=SL_E, SL_D=SL_D: s_.activation(
                        out=pf(SL_E, 512), in_=pf(SL_D, 512), func=AF.Gelu,
                        scale=cb[:, CB_1:CB_1 + 1], bias=cb[:, CB_NY0:CB_NY0 + 1]),
                        waits=[sat], inc=True)
                    y1t = p.emit("S", lambda s_, SL_F=SL_F, SL_D=SL_D: s_.activation(
                        out=pf(SL_F, 512), in_=pf(SL_D, 512), func=AF.Gelu,
                        scale=cb[:, CB_1:CB_1 + 1], bias=cb[:, CB_NY1:CB_NY1 + 1]), inc=True)
                    dtk = p.emit("G", lambda g_, SL_A=SL_A, SL_F=SL_F, SL_E=SL_E: g_.tensor_tensor(
                        out=pf(SL_A, 512), in0=pf(SL_F, 512), in1=pf(SL_E, 512),
                        op=ALU.subtract),
                        waits=[y1t, chain_tok.get(pr - 2)], inc=True)
                    m4t = p.emit("V", lambda v, SL_C=SL_C, SL_D=SL_D, SL_B=SL_B: v.scalar_tensor_tensor(
                        out=pf(SL_C, 512), in0=pf(SL_D, 512), scalar=-1024.0,
                        in1=pf(SL_B, 512), op0=ALU.mult, op1=ALU.add),
                        waits=[sat], inc=True)
                    p4t = p.emit("G", lambda g_, SL_B=SL_B, SL_C=SL_C, SL_A=SL_A: g_.tensor_tensor(
                        out=pf(SL_B, 512), in0=pf(SL_C, 512), in1=pf(SL_A, 512),
                        op=ALU.mult),
                        waits=[m4t], inc=True)
                    back_p4[pr] = p4t
                    if pr >= 1:
                        emit_back(pr - 1)
                    att_late(it)

                def emit_final(po_done=po_done, g8=g8, b=b):
                    fin_toks = []
                    xt2s = {}
                    bt = None
                    for s2 in range(2):
                        for half in range(2):
                            bank = (psA, psB, psC, psD)[s2 * 2 + half]
                            bt = p.emit("T", lambda tt, bank=bank, half=half: tt.matmul(
                                bank[:], lhsT=onesf[:, 0:128],
                                rhs=b2r[:, half * 512:half * 512 + 512],
                                start=False, stop=True),
                                waits=[po_done] if (s2 == 0 and half == 0) else (),
                                inc=(s2 == 1 and half == 1))
                    for s2 in (1, 0):
                        b0 = psA if s2 == 0 else psC
                        b1_ = psB if s2 == 0 else psD
                        xi2ap = poolF[:, XI2 + s2 * 1024:XI2 + (s2 + 1) * 1024]
                        p.emit("V", lambda v, b0=b0, xi2ap=xi2ap: v.tensor_scalar(
                            out=xi2ap[:, 0:512], in0=b0[:], scalar1=256.0, scalar2=M1,
                            op0=ALU.mult, op1=ALU.add),
                               waits=[bt, store_tok.get(s2)])
                        p.emit("V", lambda v, b1_=b1_, xi2ap=xi2ap: v.tensor_scalar(
                            out=xi2ap[:, 512:1024], in0=b1_[:], scalar1=256.0, scalar2=M1,
                            op0=ALU.mult, op1=ALU.add))
                        xt2 = p.emit("V", lambda v, xi2ap=xi2ap: v.tensor_scalar(
                            out=xi2ap, in0=xi2ap, scalar1=M1, scalar2=None,
                            op0=ALU.subtract), inc=True)
                        fin_toks.append(xt2)
                        xt2s[s2] = xt2

                    def rest():
                        for s2 in (1, 0):
                            xi2ap = poolF[:, XI2 + s2 * 1024:XI2 + (s2 + 1) * 1024]
                            _, rtok = ln_stats([xi2ap], cb[:, CB_EPS4:CB_EPS4 + 1],
                                               stat(S_R), stat(S_NMR),
                                               wait_first=[xt2s[s2]])
                            p.emit("V", lambda v, xi2ap=xi2ap: v.tensor_scalar(
                                out=xi2ap, in0=xi2ap, scalar1=stat(S_R),
                                scalar2=stat(S_NMR), op0=ALU.mult, op1=ALU.add),
                                waits=[rtok])
                            outap = xi2ap
                            ot = p.emit("V", lambda v, s2=s2, outap=outap: v.tensor_tensor(
                                out=outap, in0=outap,
                                in1=x2b[:, s2 * 1024:(s2 + 1) * 1024], op=ALU.add),
                                inc=True)
                            tok0 = g8 * 256 + s2 * 128
                            eng = "Y" if s2 == 0 else "G"
                            store_tok[s2] = p.emit(
                                eng, lambda h, b=b, tok0=tok0, outap=outap: h.dma_start(
                                    out=yout[b, tok0:tok0 + 128, :], in_=outap),
                                waits=[ot], inc="S%d" % s2)
                    return fin_toks, rest
                pending_final = emit_final
                prev_xbuf_toks = x2_toks + ln1_ntoks
                prev_h_ev = [h_ev.get(14), h_ev.get(15)]


        if pending_final is not None:
            _ft, _frest = pending_final()
            _frest()
            pending_final = None

        f0, f1 = p.count["S0"], p.count["S1"]
        p.ops["Y"].append(lambda h, f0=f0, f1=f1: (
            h.wait_ge(p.sem["S0"], f0) if f0 else None,
            h.wait_ge(p.sem["S1"], f1) if f1 else None))

        block = es.enter_context(nc.Block())

        @block.tensor
        def _(t):
            for f in p.ops["T"]:
                f(t)

        @block.vector
        def _(v):
            for f in p.ops["V"]:
                f(v)

        @block.scalar
        def _(s):
            for f in p.ops["S"]:
                f(s)

        @block.gpsimd
        def _(g):
            for f in p.ops["G"]:
                f(g)

        @block.sync
        def _(y):
            for f in p.ops["Y"]:
                f(y)

    return nc


def _host_inputs(inputs):
    f32 = np.float32
    asc = np.ascontiguousarray
    Wq = inputs["L0_attn_Wq"]; Wk = inputs["L0_attn_Wk"]
    Wv = inputs["L0_attn_Wv"]; Wo = inputs["L0_attn_Wo"]
    P_ = np.asarray(inputs["L0_attn_P"], f32)
    pbd = np.zeros((128, 128), f32)
    pbd[0:64, 0:64] = P_.T
    pbd[64:128, 64:128] = P_.T
    e16 = np.zeros((16, D), f32)
    for h in range(16):
        e16[h, h * 64:(h + 1) * 64] = 1.0
    cbv = np.zeros(NCB, f32)
    cbv[:len(CB_VALS)] = CB_VALS
    common = {
        "wq_t": asc(np.asarray(Wq, f32).T), "wk_t": asc(np.asarray(Wk, f32).T),
        "wv_t": asc(np.asarray(Wv, f32).T), "wo_t": asc(np.asarray(Wo, f32).T),
        "w1_d": asc(np.asarray(inputs["L0_ffn_w1"], f32)),
        "w2_d": asc(np.asarray(inputs["L0_ffn_w2"], f32)),
        "pbd_d": pbd,
        "bql_d": asc(np.asarray(inputs["L0_attn_bq"], f32).reshape(8, 128).T),
        "b1l_d": asc(np.asarray(inputs["L0_ffn_b1"], f32).reshape(32, 128).T),
        "bkl_d": asc(np.asarray(inputs["L0_attn_bk"], f32).reshape(8, 128).T),
        "bvr_d": np.asarray(inputs["L0_attn_bv"], f32).reshape(1, D),
        "bor_d": np.asarray(inputs["L0_attn_bo"], f32).reshape(1, D),
        "b1r_d": np.asarray(inputs["L0_ffn_b1"], f32).reshape(1, FF),
        "b2r_d": np.asarray(inputs["L0_ffn_b2"], f32).reshape(1, D),
        "cb_d": asc(np.broadcast_to(cbv, (128, NCB))),
        "idb_d": np.eye(128, dtype=f32).astype(BFNP),
        "idf_d": np.eye(128, dtype=f32),
        "e16_d": e16,
        "onesf_d": np.ones((1, 512), f32),
        "onesb_d": np.ones((128, 1), f32).astype(BFNP),
    }
    x = asc(np.asarray(inputs["L0_attn_x"], f32))
    in_maps = []
    for i in range(NCORES):
        m = dict(common)
        m["xin"] = x[i * BCORE:(i + 1) * BCORE]
        in_maps.append(m)
    return in_maps


def kernel(**inputs):
    in_maps = _host_inputs(inputs)
    nc = build_program()
    res = run_bass_kernel_spmd(nc, in_maps, core_ids=list(range(NCORES)))
    return np.concatenate([np.asarray(r["yout"]) for r in res.results], axis=0)


